# version 21
# speedup vs baseline: 1.0891x; 1.0612x over previous
import contextlib
import numpy as np
import ml_dtypes
import concourse.bass as bass
import concourse.mybir as mybir
from concourse.bass_utils import run_bass_kernel_spmd

F32 = mybir.dt.float32
BF16 = mybir.dt.bfloat16
AF = mybir.ActivationFunctionType
OP = mybir.AluOpType
AX = mybir.AxisListType
U32 = mybir.dt.uint32

D = 1024
TC = 4096
NB = 2
SEQ = 2048
C = 64
NH = 8
HD = 128
FF = 4096
NE = 8
NT = D // 128
CH = 512
NCH = TC // CH

BF = ml_dtypes.bfloat16


class Sem:
    def __init__(self, ctx, nc, name):
        pool = getattr(nc, "_sem_ctx", None)
        tgt = pool if pool is not None else ctx
        self.h = tgt.enter_context(nc.semaphore(name))
        self.n = 0


def fmv(hbm, c0, c1):
    return hbm.rearrange("(t p) n -> p t n", p=128)[:, :, c0:c1]


def sbv(t, c0, c1, nt=NT, w=None):
    w = w if w is not None else (t.shape[1] // nt)
    return t.rearrange("p (t n) -> p t n", t=nt)[:, :, c0:c1]


def emit_ln(nc, cst, y_hbm, xprev_hbm, xout_hbm, xbf_hbm, eps=1e-5, name="ln"):
    with contextlib.ExitStack() as ctx:
        sb = ctx.enter_context
        yb = sb(nc.sbuf_tensor(name + "_y", [128, 2 * NT * CH], F32))
        xpb = sb(nc.sbuf_tensor(name + "_xp", [128, 2 * NT * CH], F32)) if xprev_hbm is not None else None
        tb = sb(nc.sbuf_tensor(name + "_t", [128, NT * CH], F32))
        tbf = sb(nc.sbuf_tensor(name + "_tbf", [128, NT * CH], BF16))
        sqb = sb(nc.sbuf_tensor(name + "_sq", [128, NT * CH], BF16))
        srow = sb(nc.sbuf_tensor(name + "_srow", [1, 6 * CH], F32))
        xob = sb(nc.sbuf_tensor(name + "_xo", [128, NT * CH], F32))
        xbb = sb(nc.sbuf_tensor(name + "_xb", [128, NT * CH], BF16))
        epst = sb(nc.sbuf_tensor(name + "_eps", [1, 1], F32))
        ps_st = sb(nc.psum_tensor(name + "_ps", [64, CH], F32))
        ps_bc = sb(nc.psum_tensor(name + "_bc", [128, 2 * CH], F32))

        s_ld = Sem(ctx, nc, name + "_ld")
        s_ld2 = Sem(ctx, nc, name + "_ld2")
        s_t = Sem(ctx, nc, name + "_t")
        s_tbf = Sem(ctx, nc, name + "_tbf")
        s_sq = Sem(ctx, nc, name + "_sq")
        s_stats = Sem(ctx, nc, name + "_st")
        s_bc = Sem(ctx, nc, name + "_bcd")
        s_var = Sem(ctx, nc, name + "_var")
        s_rstd = Sem(ctx, nc, name + "_rs")
        s_ap = Sem(ctx, nc, name + "_ap")
        s_out = Sem(ctx, nc, name + "_out")

        with nc.Block() as blk:
            @blk.vector
            def _(v0):
                v0.memset(epst[:, :], eps)

            @blk.sync
            def _(s):
                for c in range(NCH):
                    par = c % 2
                    if c >= 2:
                        s.wait_ge(s_t.h, c - 1)
                    s.dma_start(out=yb[:, par * NT * CH:(par + 1) * NT * CH].rearrange("p (t n) -> p t n", t=NT),
                                in_=fmv(y_hbm, c * CH, (c + 1) * CH)).then_inc(s_ld.h, 16)

            @blk.vector
            def _(v):
                for c in range(NCH):
                    par = c % 2
                    v.wait_ge(s_ld.h, 16 * (c + 1))
                    if xpb is not None:
                        v.wait_ge(s_ld2.h, 16 * (c + 1))
                    if c > 0:
                        v.wait_ge(s_sq.h, c)
                        v.wait_ge(s_stats.h, c)
                    ysel = yb[:, par * NT * CH:(par + 1) * NT * CH]
                    if xpb is not None:
                        xsel = xpb[:, par * NT * CH:(par + 1) * NT * CH]
                        v.tensor_tensor(tb[:, :], ysel, xsel, OP.add).then_inc(s_t.h, 1)
                    else:
                        v.tensor_copy(tb[:, :], ysel).then_inc(s_t.h, 1)
                    v.tensor_copy(tbf[:, :], tb[:, :]).then_inc(s_tbf.h, 1)
                    v.wait_ge(s_stats.h, c + 1)
                    v.tensor_scalar(out=srow[:, 0:CH], in0=ps_st[0:1, :], scalar1=1.0 / D,
                                    scalar2=None, op0=OP.mult)
                    v.drain()
                    v.tensor_tensor(srow[:, 2 * CH:3 * CH], srow[:, 0:CH], srow[:, 0:CH], OP.mult)
                    v.drain()
                    v.scalar_tensor_tensor(out=srow[:, CH:2 * CH], in0=ps_st[32:33, :],
                                           scalar=1.0 / D, in1=srow[:, 2 * CH:3 * CH],
                                           op0=OP.mult, op1=OP.subtract).then_inc(s_var.h, 1)
                    v.wait_ge(s_bc.h, c + 1)
                    if c > 0:
                        v.wait_ge(s_out.h, (32 if xbf_hbm is not None else 16) * c)
                    for i in range(NT):
                        v.tensor_tensor(xob[:, i * CH:(i + 1) * CH], tb[:, i * CH:(i + 1) * CH],
                                        ps_bc[:, 0:CH], OP.subtract)
                    for i in range(NT):
                        ttl = v.tensor_tensor(xob[:, i * CH:(i + 1) * CH], xob[:, i * CH:(i + 1) * CH],
                                              ps_bc[:, CH:2 * CH], OP.mult)
                    if xbf_hbm is not None:
                        v.tensor_copy(xbb[:, :], xob[:, :]).then_inc(s_ap.h, 1)
                    else:
                        ttl.then_inc(s_ap.h, 1)

            @blk.scalar
            def _(sc):
                for c in range(NCH):
                    par = c % 2
                    if xpb is not None:
                        if c >= 2:
                            sc.wait_ge(s_t.h, c - 1)
                        sc.dma_start(out=xpb[:, par * NT * CH:(par + 1) * NT * CH].rearrange("p (t n) -> p t n", t=NT),
                                     in_=fmv(xprev_hbm, c * CH, (c + 1) * CH)).then_inc(s_ld2.h, 16)
                    sc.wait_ge(s_t.h, c + 1)
                    if c > 0:
                        sc.wait_ge(s_stats.h, c)
                    sc.activation(out=sqb[:, :], in_=tb[:, :], func=AF.Square).then_inc(s_sq.h, 1)
                    sc.wait_ge(s_var.h, c + 1)
                    sc.activation(out=srow[:, 3 * CH:4 * CH], in_=srow[:, CH:2 * CH],
                                  func=AF.Ln, bias=epst[0:1, 0:1])
                    sc.activation(out=srow[:, 4 * CH:5 * CH], in_=srow[:, 3 * CH:4 * CH],
                                  func=AF.Exp, scale=-0.5).then_inc(s_rstd.h, 1)

            @blk.tensor
            def _(t):
                for c in range(NCH):
                    t.wait_ge(s_tbf.h, c + 1)
                    t.wait_ge(s_sq.h, c + 1)
                    if c > 0:
                        t.wait_ge(s_var.h, c)
                    for i in range(NT):
                        t.matmul(ps_st[0:1, :], cst["onesc"][:, :], tbf[:, i * CH:(i + 1) * CH],
                                 start=(i == 0), stop=(i == NT - 1))
                    for i in range(NT):
                        mm = t.matmul(ps_st[32:33, :], cst["onesc"][:, :], sqb[:, i * CH:(i + 1) * CH],
                                      start=(i == 0), stop=(i == NT - 1))
                    mm.then_inc(s_stats.h, 1)
                    t.wait_ge(s_rstd.h, c + 1)
                    if c > 0:
                        t.wait_ge(s_ap.h, c)
                    t.matmul(ps_bc[:, 0:CH], cst["onesrf"][:, :], srow[:, 0:CH], start=True, stop=True)
                    t.matmul(ps_bc[:, CH:2 * CH], cst["onesrf"][:, :], srow[:, 4 * CH:5 * CH],
                             start=True, stop=True).then_inc(s_bc.h, 1)

            @blk.gpsimd
            def _(g):
                nst = 32 if xbf_hbm is not None else 16
                for c in range(NCH):
                    g.wait_ge(s_ap.h, c + 1)
                    g.dma_start(out=fmv(xout_hbm, c * CH, (c + 1) * CH),
                                in_=xob.rearrange("p (t n) -> p t n", t=NT)).then_inc(s_out.h, 16)
                    if xbf_hbm is not None:
                        g.dma_start(out=fmv(xbf_hbm, c * CH, (c + 1) * CH),
                                    in_=xbb.rearrange("p (t n) -> p t n", t=NT)).then_inc(s_out.h, 16)
                g.wait_ge(s_out.h, nst * NCH)


def emit_bias_cols(nc, cst, src_hbm, n, out_sb, out_col0, name):
    ctx = contextlib.ExitStack()
    ctx.__enter__()
    s_l = Sem(ctx, nc, name + "_l")
    s_m = Sem(ctx, nc, name + "_m")
    s_c = Sem(ctx, nc, name + "_c")
    row = ctx.enter_context(nc.sbuf_tensor(name + "_row", [max(n, 16), 128], F32))
    pst = ctx.enter_context(nc.psum_tensor(name + "_ps", [128, max(n, 2)], F32))
    with nc.Block() as blk:
        @blk.sync
        def _(s):
            s.dma_start(out=row[0:n, :], in_=src_hbm.rearrange("(t p) -> t p", p=128)).then_inc(s_l.h, 16)

        @blk.tensor
        def _(t):
            t.wait_ge(s_l.h, 16)
            t.transpose(pst[:, 0:n], row[0:n, :], cst["ident"][0:n, 0:n]).then_inc(s_m.h, 1)

        @blk.vector
        def _(v):
            v.wait_ge(s_m.h, 1)
            v.tensor_copy(out_sb[:, out_col0:out_col0 + n], pst[:, 0:n]).then_inc(s_c.h, 1)

        @blk.gpsimd
        def _(g):
            g.wait_ge(s_c.h, 1)
    ctx.__exit__(None, None, None)


def emit_binarize(nc, ctx, w_hbm, rows, cols, scr_hbm, name):
    ntr = rows // 128
    with contextlib.ExitStack() as c2:
        per = max(1, min(ntr, 8192 // cols))
        wn = c2.enter_context(nc.sbuf_tensor(name + "_wn", [128, 2 * per * cols], BF16))
        asum = c2.enter_context(nc.sbuf_tensor(name + "_as", [128, ntr], F32))
        s_wl = Sem(c2, nc, name + "_wl")
        s_red = Sem(c2, nc, name + "_red")
        s_sgn = Sem(c2, nc, name + "_sgn")
        s_ts = Sem(c2, nc, name + "_ts")
        s_scr = Sem(c2, nc, name + "_scr")
        ngrp = (ntr + per - 1) // per

        def wv(gi, i):
            par = gi % 2
            return wn[:, (par * per + i) * cols:(par * per + i + 1) * cols]

        with nc.Block() as blk:
            @blk.gpsimd
            def _(g):
                for gi in range(ngrp):
                    if gi >= 2:
                        g.wait_ge(s_scr.h, 16 * (gi - 1))
                    g.dma_start(
                        out=wn[:, (gi % 2) * per * cols:((gi % 2) + 1) * per * cols].rearrange("p (t n) -> p t n", t=per),
                        in_=w_hbm.rearrange("(t p) n -> p t n", p=128)[:, gi * per:(gi + 1) * per, :],
                    ).then_inc(s_wl.h, 16)

            @blk.sync
            def _(s):
                for gi in range(ngrp):
                    s.wait_ge(s_ts.h, (gi + 1) * per)
                    s.dma_start(
                        out=scr_hbm.rearrange("(t p) n -> p t n", p=128)[:, gi * per:(gi + 1) * per, :],
                        in_=wn[:, (gi % 2) * per * cols:((gi % 2) + 1) * per * cols].rearrange("p (t n) -> p t n", t=per),
                    ).then_inc(s_scr.h, 16)
                s.wait_ge(s_scr.h, 16 * ngrp)

            @blk.vector
            def _(v):
                for gi in range(ngrp):
                    for i in range(per):
                        ti = gi * per + i
                        if i == 0:
                            v.wait_ge(s_wl.h, 16 * (gi + 1))
                        v.tensor_reduce(out=asum[:, ti:ti + 1], in_=wv(gi, i),
                                        axis=AX.X, op=OP.add, apply_absolute_value=True).then_inc(s_red.h, 1)
                    for i in range(per):
                        ti = gi * per + i
                        v.wait_ge(s_sgn.h, ti + 1)
                        v.tensor_scalar(out=wv(gi, i), in0=wv(gi, i),
                                        scalar1=asum[:, ti:ti + 1], scalar2=1.0 / cols,
                                        op0=OP.mult, op1=OP.mult).then_inc(s_ts.h, 1)

            @blk.scalar
            def _(sc):
                for gi in range(ngrp):
                    for i in range(per):
                        ti = gi * per + i
                        sc.wait_ge(s_red.h, ti + 1)
                        sc.activation(out=wv(gi, i), in_=wv(gi, i), func=AF.Sign).then_inc(s_sgn.h, 1)


def emit_l0ffn(nc, cst, xbf_hbm, w1_hbm, b1_hbm, w2_hbm, b2_hbm, y_hbm, scr1_hbm, scr2_hbm,
               skip_binarize=False):
    with contextlib.ExitStack() as ctx:
        if not skip_binarize:
            emit_binarize(nc, ctx, w1_hbm, FF, D, scr1_hbm, "bz1")
            emit_binarize(nc, ctx, w2_hbm, D, FF, scr2_hbm, "bz2")

        sb = ctx.enter_context
        bcols = sb(nc.sbuf_tensor("f_bc", [128, 40], F32))
        emit_bias_cols(nc, cst, b1_hbm, 32, bcols, 0, "fb1")
        emit_bias_cols(nc, cst, b2_hbm, 8, bcols, 32, "fb2")

        bw1T = sb(nc.sbuf_tensor("f_bw1T", [128, NT * FF], BF16))
        bw2T = sb(nc.sbuf_tensor("f_bw2T", [128, 32 * 1024], BF16))
        hsb = sb(nc.sbuf_tensor("f_h", [128, 32 * CH], BF16))
        xc = sb(nc.sbuf_tensor("f_xc", [128, 2 * NT * CH], BF16))
        ysb = sb(nc.sbuf_tensor("f_ysb", [128, NT * CH], F32))
        ph = sb(nc.psum_tensor("f_ph", [128, 4 * CH], F32))
        po = sb(nc.psum_tensor("f_po", [128, 4 * CH], F32))

        s_tr = Sem(ctx, nc, "f_tr")
        s_x = Sem(ctx, nc, "f_x")
        s_hmm = Sem(ctx, nc, "f_hmm")
        s_hev = Sem(ctx, nc, "f_hev")
        s_ogrp = Sem(ctx, nc, "f_ogrp")
        s_oev = Sem(ctx, nc, "f_oev")
        s_y = Sem(ctx, nc, "f_ys")

        with nc.Block() as blk:
            @blk.sync
            def _(s):
                for i in range(NT):
                    s.dma_start_transpose(bw1T[:, i * FF:(i + 1) * FF],
                                          scr1_hbm[:, i * 128:(i + 1) * 128]).then_inc(s_tr.h, 16)
                for i in range(32):
                    s.dma_start_transpose(bw2T[:, i * 1024:(i + 1) * 1024],
                                          scr2_hbm[:, i * 128:(i + 1) * 128]).then_inc(s_tr.h, 16)
                for c in range(NCH):
                    par = c % 2
                    if c >= 2:
                        s.wait_ge(s_hmm.h, (c - 1) * 16)
                    s.dma_start(
                        out=xc[:, par * NT * CH:(par + 1) * NT * CH].rearrange("p (t n) -> p t n", t=NT),
                        in_=fmv(xbf_hbm, c * CH, (c + 1) * CH)).then_inc(s_x.h, 16)

            @blk.tensor
            def _(t):
                t.wait_ge(s_tr.h, 16 * 8)
                for c in range(NCH):
                    par = c % 2
                    t.wait_ge(s_x.h, 16 * (c + 1))
                    if c == 0:
                        pass
                    for grp in range(16):
                        pg = grp % 2
                        if c * 16 + grp >= 2:
                            t.wait_ge(s_hev.h, 2 * (c * 16 + grp - 2) + 2)
                        for m2 in range(2):
                            f = grp * 2 + m2
                            for k in range(NT):
                                mm = t.matmul(ph[:, (pg * 2 + m2) * CH:(pg * 2 + m2 + 1) * CH],
                                              bw1T[:, k * FF + f * 128:k * FF + (f + 1) * 128],
                                              xc[:, (par * NT + k) * CH:(par * NT + k + 1) * CH],
                                              start=(k == 0), stop=(k == NT - 1))
                        mm.then_inc(s_hmm.h, 1)
                    t.wait_ge(s_hev.h, 32 * (c + 1))
                    if c == 0:
                        t.wait_ge(s_tr.h, 16 * 40)
                    for dh in range(2):
                        if 2 * c + dh > 0:
                            t.wait_ge(s_oev.h, 2 * c + dh)
                        for m in range(4):
                            dt_ = dh * 4 + m
                            for k in range(32):
                                mm = t.matmul(po[:, m * CH:(m + 1) * CH],
                                              bw2T[:, k * 1024 + dt_ * 128:k * 1024 + (dt_ + 1) * 128],
                                              hsb[:, k * CH:(k + 1) * CH],
                                              start=(k == 0), stop=(k == 31))
                        mm.then_inc(s_ogrp.h, 1)

            @blk.scalar
            def _(sc):
                for c in range(NCH):
                    for grp in range(16):
                        pg = grp % 2
                        sc.wait_ge(s_hmm.h, c * 16 + grp + 1)
                        for m2 in range(2):
                            f = grp * 2 + m2
                            sc.activation(out=hsb[:, f * CH:(f + 1) * CH],
                                          in_=ph[:, (pg * 2 + m2) * CH:(pg * 2 + m2 + 1) * CH],
                                          func=AF.Gelu, bias=bcols[:, f:f + 1],
                                          scale=1.0).then_inc(s_hev.h, 1)

            @blk.vector
            def _(v):
                for c in range(NCH):
                    for dh in range(2):
                        v.wait_ge(s_ogrp.h, 2 * c + dh + 1)
                        if c > 0 and dh == 0:
                            v.wait_ge(s_y.h, 16 * c)
                        for m in range(4):
                            dt_ = dh * 4 + m
                            ts = v.tensor_scalar(out=ysb[:, dt_ * CH:(dt_ + 1) * CH],
                                                 in0=po[:, m * CH:(m + 1) * CH],
                                                 scalar1=bcols[:, 32 + dt_:32 + dt_ + 1],
                                                 scalar2=None, op0=OP.add)
                        ts.then_inc(s_oev.h, 1)

            @blk.gpsimd
            def _(g):
                for c in range(NCH):
                    g.wait_ge(s_oev.h, 2 * (c + 1))
                    g.dma_start(out=fmv(y_hbm, c * CH, (c + 1) * CH),
                                in_=ysb.rearrange("p (t n) -> p t n", t=NT)).then_inc(s_y.h, 16)
                g.wait_ge(s_y.h, 16 * NCH)


CONST_DEFS = [
    ("ident", (128, 128), np.float32),
    ("onesc", (128, 1), BF),
    ("onesr", (1, 128), BF),
    ("onesrf", (1, 128), np.float32),
    ("sel10", (10, 1280), BF),
        ("iota2_c", (128, 1), np.float32),
    ("iota2_cm1", (128, 1), np.float32),
    ("identb", (128, 128), BF),
    ("dm1T2", (128, 64), np.float32),
    ("striu2", (128, 64), np.float32),
    ("iota2_p1", (128, 1), np.float32),
    ("dmat2", (128, 64), np.float32),
    ("dmat2T", (128, 64), np.float32),
    ("triu2v", (128, 64), np.float32),
    ("rowc1", (128, 64), np.float32),
    ("rowcm1", (128, 64), np.float32),
    ("eye2", (128, 64), BF),
    ("eye2r", (128, 512), BF),
    ("stril2", (128, 64), np.float32),
    ("eoff8", (128, 8), np.float32),
    ("tokid32", (128, 32), np.float32),
]


def _sel10():
    s = np.zeros((10, 1280), dtype=BF)
    for e in range(10):
        s[e, e * 128:(e + 1) * 128] = 1
    return s


def host_constants():
    return {
        "ident": np.eye(128, dtype=np.float32),
        "identb": np.eye(128, dtype=BF),
        "onesc": np.ones((128, 1), dtype=BF),
        "onesr": np.ones((1, 128), dtype=BF),
        "onesrf": np.ones((1, 128), dtype=np.float32),
        "sel10": _sel10(),
        "iota2_c": (np.arange(128) % 64).reshape(128, 1).astype(np.float32),
        "iota2_cm1": (63 - np.arange(128) % 64).reshape(128, 1).astype(np.float32),
        "dm1T2": np.tile(np.maximum(np.arange(64)[None, :] - 1 - np.arange(64)[:, None], 0), (2, 1)).astype(np.float32),
        "striu2": np.tile(np.triu(np.ones((64, 64), np.float32), 1), (2, 1)),
        "iota2_p1": (np.arange(128) % 64 + 1).reshape(128, 1).astype(np.float32),
        "dmat2": np.tile(np.arange(64)[:, None] - np.arange(64)[None, :], (2, 1)).astype(np.float32),
        "dmat2T": np.tile(np.arange(64)[None, :] - np.arange(64)[:, None], (2, 1)).astype(np.float32),
        "triu2v": np.tile(np.triu(np.ones((64, 64), np.float32)), (2, 1)),
        "rowc1": np.tile((np.arange(64) + 1)[None, :], (128, 1)).astype(np.float32),
        "rowcm1": np.tile((63 - np.arange(64))[None, :], (128, 1)).astype(np.float32),
        "eye2": np.tile(np.eye(64, dtype=np.float32), (2, 1)).astype(BF),
        "eye2r": np.tile(np.tile(np.eye(64, dtype=np.float32), (2, 1)), (1, 8)).astype(BF),
        "stril2": np.tile(np.tril(np.ones((64, 64), np.float32), -1), (2, 1)),
        "eoff8": np.tile((np.arange(8) * 1536.0)[None, :], (128, 1)).astype(np.float32),
        "tokid32": (np.arange(128)[:, None] + 128 * np.arange(32)[None, :]).astype(np.float32),
    }


def emit_load_constants(nc, ctx, names=None, tag=""):
    if not hasattr(nc, "_cst_aps"):
        nc._cst_aps = {}
        for name, shape, dt in CONST_DEFS:
            mdt = F32 if dt == np.float32 else BF16
            nc._cst_aps[name] = nc.dram_tensor("cst_" + name, list(shape), mdt, kind="ExternalInput").ap()
    if names is None:
        names = [n for n, _, _ in CONST_DEFS]
    cst = {}
    for name, shape, dt in CONST_DEFS:
        if name not in names:
            continue
        mdt = F32 if dt == np.float32 else BF16
        cst[name] = ctx.enter_context(nc.sbuf_tensor("c" + tag + "_" + name, list(shape), mdt))
    s0 = Sem(ctx, nc, "cst_ld" + tag)
    with nc.Block() as blk:
        @blk.gpsimd
        def _(g):
            n = 0
            for name in cst:
                g.dma_start(out=cst[name][:, :], in_=nc._cst_aps[name][:, :]).then_inc(s0.h, 16)
                n += 16
            g.wait_ge(s0.h, n)
    return cst


def build_l0ffn_launch():
    nc = bass.Bass("TRN2", target_bir_lowering=False)
    xbf = nc.dram_tensor("xbf", [D, TC], BF16, kind="ExternalInput").ap()
    xprev = nc.dram_tensor("xprev", [D, TC], F32, kind="ExternalInput").ap()
    w1 = nc.dram_tensor("w1", [FF, D], F32, kind="ExternalInput").ap()
    b1 = nc.dram_tensor("b1", [FF], F32, kind="ExternalInput").ap()
    w2 = nc.dram_tensor("w2", [D, FF], F32, kind="ExternalInput").ap()
    b2 = nc.dram_tensor("b2", [D], F32, kind="ExternalInput").ap()
    xout = nc.dram_tensor("xout", [D, TC], F32, kind="ExternalOutput").ap()
    xbfo = nc.dram_tensor("xbfo", [D, TC], BF16, kind="ExternalOutput").ap()
    scr1 = nc.dram_tensor("scr1", [FF, D], BF16, kind="Internal").ap()
    scr2 = nc.dram_tensor("scr2", [D, FF], BF16, kind="Internal").ap()
    y = nc.dram_tensor("yint", [D, TC], F32, kind="Internal").ap()

    with contextlib.ExitStack() as ctx:
        cst = emit_load_constants(nc, ctx)
        emit_l0ffn(nc, cst, xbf, w1, b1, w2, b2, y, scr1, scr2)
        emit_ln(nc, cst, y, xprev, xout, xbfo, name="ln2")
    return nc


if __name__ == "__main__":
    nc = build_l0ffn_launch()
    print("built; instructions:", nc.n_instructions())


def emit_l1ffn(nc, cst, xbf_hbm, wts, y_hbm, acc_hbm, dbg_cw=None, dbg_gl=None, wbf=None, s_pc=None):
    with contextlib.ExitStack() as ctx:
        sb = ctx.enter_context
        cwrp = sb(nc.sbuf_tensor("m_cwrp", [10, TC], BF16))
        d1bcols = sb(nc.sbuf_tensor("m_d1b", [128, 32], F32))
        d2bcols = sb(nc.sbuf_tensor("m_d2b", [128, 8], F32))
        emit_bias_cols(nc, cst, wts["d1b"], 32, d1bcols, 0, "md1b")
        emit_bias_cols(nc, cst, wts["d2b"], 8, d2bcols, 0, "md2b")

        with contextlib.ExitStack() as c1:
            wgr = c1.enter_context(nc.sbuf_tensor("m_wgr", [128, NT * 10], BF16))
            bgr = c1.enter_context(nc.sbuf_tensor("m_bgr", [1, 10], F32))
            bgrb = c1.enter_context(nc.sbuf_tensor("m_bgrb", [128, 10], F32))
            xg = c1.enter_context(nc.sbuf_tensor("m_xg", [128, 2 * NT * CH], BF16))
            gl = c1.enter_context(nc.sbuf_tensor("m_gl", [128, 32 * 10], F32))
            rowb = c1.enter_context(nc.sbuf_tensor("m_rb", [128, 8 * 32], F32))
            mtmp = c1.enter_context(nc.sbuf_tensor("m_mt", [128, 20], F32))
            pgl = c1.enter_context(nc.psum_tensor("m_pgl", [128, 2 * CH], F32))
            pbc = c1.enter_context(nc.psum_tensor("m_pbc", [128, 16], F32))
            ptr = c1.enter_context(nc.psum_tensor("m_ptr", [16, 128], F32))

            s_wg = Sem(c1, nc, "m_wg")
            s_xg = Sem(c1, nc, "m_xg")
            s_gm = Sem(c1, nc, "m_gm")
            s_ge = Sem(c1, nc, "m_ge")
            s_top = Sem(c1, nc, "m_top")
            s_bgr = Sem(c1, nc, "m_bgr2")
            s_fin = Sem(c1, nc, "m_fin")

            with nc.Block() as blk:
                @blk.gpsimd
                def _(g):
                    g.dma_start(out=wgr.rearrange("p (t n) -> p t n", t=NT)[:, :, 0:8],
                                in_=wts["Wg"].rearrange("(t p) e -> p t e", p=128)).then_inc(s_wg.h, 16)
                    g.dma_start(out=wgr.rearrange("p (t n) -> p t n", t=NT)[:, :, 8:10],
                                in_=wts["Wr"].rearrange("(t p) e -> p t e", p=128)).then_inc(s_wg.h, 16)
                    g.dma_start(out=bgr[:, 0:8], in_=wts["bg"].rearrange("(o e) -> o e", o=1)).then_inc(s_wg.h, 16)
                    g.dma_start(out=bgr[:, 8:10], in_=wts["br"].rearrange("(o e) -> o e", o=1)).then_inc(s_wg.h, 16)
                    if dbg_cw is not None:
                        g.wait_ge(s_fin.h, 65)
                        g.dma_start(out=dbg_cw[:, :], in_=cwrp[:, :]).then_inc(s_wg.h, 16)
                        if dbg_gl is not None:
                            g.dma_start(out=dbg_gl[:, 0:320], in_=gl[:, :]).then_inc(s_wg.h, 16)
                            g.dma_start(out=dbg_gl[:, 320:576], in_=rowb[:, 0:256]).then_inc(s_wg.h, 16)
                        g.wait_ge(s_wg.h, 80 + (32 if dbg_gl is not None else 0))

                @blk.sync
                def _(s):
                    for q in range(NCH):
                        par = q % 2
                        if q >= 2:
                            s.wait_ge(s_gm.h, 1 + (q - 1) * 4)
                        s.dma_start(
                            out=xg[:, par * NT * CH:(par + 1) * NT * CH].rearrange("p (t n) -> p t n", t=NT),
                            in_=fmv(xbf_hbm, q * CH, (q + 1) * CH)).then_inc(s_xg.h, 16)

                @blk.tensor
                def _(t):
                    t.wait_ge(s_wg.h, 64)
                    t.matmul(pbc[:, 0:10], cst["onesrf"][:, :], bgr[:, :], start=True, stop=True).then_inc(s_gm.h, 1)
                    for ti in range(32):
                        q, j = ti // 4, ti % 4
                        par = q % 2
                        pp = ti % 2
                        t.wait_ge(s_xg.h, 16 * (q + 1))
                        if ti >= 2:
                            t.wait_ge(s_ge.h, ti - 1)
                        for k in range(NT):
                            mm = t.matmul(pgl[:, pp * CH:pp * CH + 10],
                                          xg[:, (par * NT + k) * CH + j * 128:(par * NT + k) * CH + (j + 1) * 128],
                                          wgr[:, k * 10:(k + 1) * 10],
                                          start=(k == 0), stop=(k == NT - 1))
                        mm.then_inc(s_gm.h, 1)
                    for ti in range(32):
                        t.wait_ge(s_fin.h, ti + 1)
                        if ti > 0:
                            t.wait_ge(s_fin.h, 32 + ti)
                        t.transpose(ptr[0:10, :], gl[:, ti * 10:(ti + 1) * 10],
                                    cst["ident"][:, :]).then_inc(s_gm.h, 1)

                @blk.scalar
                def _(sc):
                    sc.wait_ge(s_gm.h, 1)
                    sc.copy(bgrb[:, :], pbc[:, 0:10]).then_inc(s_bgr.h, 1)
                    sc.wait_ge(s_top.h, 32)
                    sc.activation(out=rowb[:, 96:128], in_=rowb[:, 64:96], func=AF.Sigmoid)
                    sc.activation(out=rowb[:, 192:224], in_=rowb[:, 160:192],
                                  func=AF.Sigmoid).then_inc(s_bgr.h, 1)

                @blk.vector
                def _(v):
                    for ti in range(32):
                        pp = ti % 2
                        v.wait_ge(s_gm.h, 1 + ti + 1)
                        if ti == 0:
                            v.wait_ge(s_bgr.h, 1)
                        v.tensor_tensor(gl[:, ti * 10:(ti + 1) * 10], pgl[:, pp * CH:pp * CH + 10],
                                        bgrb[:, :], OP.add).then_inc(s_ge.h, 1)
                        ge = gl[:, ti * 10:ti * 10 + 8]
                        v.drain()
                        v.tensor_reduce(out=rowb[:, ti:ti + 1], in_=ge, axis=AX.X, op=OP.max)
                        v.drain()
                        v.tensor_scalar(out=mtmp[:, 0:8], in0=ge, scalar1=rowb[:, ti:ti + 1],
                                        scalar2=None, op0=OP.is_ge)
                        v.drain()
                        v.scalar_tensor_tensor(out=mtmp[:, 10:18], in0=mtmp[:, 0:8], scalar=-1e30,
                                               in1=ge, op0=OP.mult, op1=OP.add)
                        v.drain()
                        v.tensor_reduce(out=rowb[:, 32 + ti:33 + ti], in_=mtmp[:, 10:18],
                                        axis=AX.X, op=OP.max)
                        v.drain()
                        v.tensor_tensor(rowb[:, 64 + ti:65 + ti], rowb[:, ti:ti + 1],
                                        rowb[:, 32 + ti:33 + ti], OP.subtract)
                        v.tensor_tensor(rowb[:, 160 + ti:161 + ti], gl[:, ti * 10 + 8:ti * 10 + 9],
                                        gl[:, ti * 10 + 9:ti * 10 + 10], OP.subtract).then_inc(s_top.h, 1)
                    v.wait_ge(s_bgr.h, 2)
                    v.tensor_scalar(out=rowb[:, 128:160], in0=rowb[:, 96:128], scalar1=-1.0,
                                    scalar2=1.0, op0=OP.mult, op1=OP.add)
                    v.tensor_scalar(out=rowb[:, 224:256], in0=rowb[:, 192:224], scalar1=-1.0,
                                    scalar2=1.0, op0=OP.mult, op1=OP.add)
                    v.drain()
                    for ti in range(32):
                        ge = gl[:, ti * 10:ti * 10 + 8]
                        v.tensor_scalar(out=mtmp[:, 0:8], in0=ge, scalar1=rowb[:, ti:ti + 1],
                                        scalar2=None, op0=OP.is_ge)
                        v.drain()
                        v.scalar_tensor_tensor(out=mtmp[:, 10:18], in0=mtmp[:, 0:8], scalar=-1e30,
                                               in1=ge, op0=OP.mult, op1=OP.add)
                        v.drain()
                        v.tensor_scalar(out=mtmp[:, 10:18], in0=mtmp[:, 10:18],
                                        scalar1=rowb[:, 32 + ti:33 + ti], scalar2=None, op0=OP.is_ge)
                        v.drain()
                        v.tensor_scalar(out=ge, in0=mtmp[:, 0:8],
                                        scalar1=rowb[:, 96 + ti:97 + ti], scalar2=None, op0=OP.mult)
                        v.drain()
                        v.scalar_tensor_tensor(out=ge, in0=mtmp[:, 10:18],
                                               scalar=rowb[:, 128 + ti:129 + ti],
                                               in1=ge, op0=OP.mult, op1=OP.add)
                        v.drain()
                        v.tensor_copy(gl[:, ti * 10 + 8:ti * 10 + 9], rowb[:, 192 + ti:193 + ti])
                        v.tensor_copy(gl[:, ti * 10 + 9:ti * 10 + 10],
                                      rowb[:, 224 + ti:225 + ti]).then_inc(s_fin.h, 1)
                    for ti in range(32):
                        v.wait_ge(s_gm.h, 33 + ti + 1)
                        v.tensor_copy(cwrp[:, ti * 128:(ti + 1) * 128], ptr[0:10, :]).then_inc(s_fin.h, 1)
                    if dbg_cw is not None:
                        v.memset(mtmp[:, 0:1], 0.0).then_inc(s_fin.h, 1)
                    
        with contextlib.ExitStack() as c3:
            b2sb = c3.enter_context(nc.sbuf_tensor("m_b2", [8, D], BF16))
            b1cols = c3.enter_context(nc.sbuf_tensor("m_b1c", [128, NE * 32], F32))
            for e in range(NE):
                emit_bias_cols(nc, cst, wts["b1"][e, :], 32, b1cols, e * 32, f"mb1_{e}")
            accb = c3.enter_context(nc.sbuf_tensor("m_acc", [128, 2 * CH], F32))
            u1 = c3.enter_context(nc.sbuf_tensor("m_u1", [128, CH], F32))
            cwx = c3.enter_context(nc.sbuf_tensor("m_cwx", [128, CH], F32))
            w1sb = c3.enter_context(nc.sbuf_tensor("m_w1", [128, NT * FF], BF16))
            w2sb = c3.enter_context(nc.sbuf_tensor("m_w2", [128, 32 * 1024], BF16))
            hsb = c3.enter_context(nc.sbuf_tensor("m_h", [128, 32 * CH], BF16))
            xcc = c3.enter_context(nc.sbuf_tensor("m_xc", [128, 2 * NT * CH], BF16))
            tmp = c3.enter_context(nc.sbuf_tensor("m_tmp", [128, 2 * 2 * CH], F32))
            ph = c3.enter_context(nc.psum_tensor("m_ph", [128, 2 * CH], F32))
            po = c3.enter_context(nc.psum_tensor("m_po", [128, 4 * CH], F32))
            pcw = c3.enter_context(nc.psum_tensor("m_pcw", [128, 2 * CH], F32))

            s_w1 = Sem(c3, nc, "m_sw1")
            s_av = Sem(c3, nc, "m_sav")
            s_w2 = Sem(c3, nc, "m_sw2")
            s_xc = Sem(c3, nc, "m_sxc")
            s_cwm = Sem(c3, nc, "m_scw")
            s_cwv = Sem(c3, nc, "m_scwv")
            s_hm = Sem(c3, nc, "m_shm")
            s_he = Sem(c3, nc, "m_she")
            s_om = Sem(c3, nc, "m_som")
            s_fl = Sem(c3, nc, "m_sfl")
            s_ac = Sem(c3, nc, "m_sac")

            with nc.Block() as blk:
                @blk.gpsimd
                def _(g):
                    def emit_acc(e, c):
                        for dq in range(4):
                            oidx = (e * NCH + c) * 4 + dq
                            g.wait_ge(s_fl.h, oidx + 1)
                            if e == NE:
                                g.dma_start(
                                    out=y_hbm.rearrange("(t p) n -> p t n", p=128)[:, dq * 2:(dq + 1) * 2, c * CH:(c + 1) * CH],
                                    in_=tmp[:, (oidx % 2) * 2 * CH:((oidx % 2) + 1) * 2 * CH].rearrange("p (t n) -> p t n", t=2),
                                ).then_inc(s_ac.h, 16)
                            else:
                                g.dma_start(
                                    out=acc_hbm.rearrange("(t p) n -> p t n", p=128)[:, dq * 2:(dq + 1) * 2, c * CH:(c + 1) * CH],
                                    in_=tmp[:, (oidx % 2) * 2 * CH:((oidx % 2) + 1) * 2 * CH].rearrange("p (t n) -> p t n", t=2),
                                    accum_op=(OP.add if e > 0 else OP.bypass),
                                ).then_inc(s_ac.h, 16)

                    g.dma_start(out=b2sb[:, :], in_=wts["b2"][:, :]).then_inc(s_w2.h, 16)
                    for e in range(NE + 1):
                        if wbf is None:
                            w1src = (wts["D1"][:, :] if e == NE else wts["W1"][e * D:(e + 1) * D, :])
                            w2src = (wts["D2"][:, :] if e == NE else wts["W2"][e * FF:(e + 1) * FF, :])
                            if e > 0:
                                g.wait_ge(s_hm.h, e * NCH * 32)
                            for q in range(4):
                                g.dma_start(
                                    out=w1sb[:, q * 2 * FF:(q + 1) * 2 * FF].rearrange("p (t n) -> p t n", t=2),
                                    in_=w1src.rearrange("(t p) n -> p t n", p=128)[:, q * 2:(q + 1) * 2, :],
                                ).then_inc(s_w1.h, 16)
                            if e > 0:
                                g.wait_ge(s_om.h, e * NCH * 4)
                            for q in range(4):
                                g.dma_start(
                                    out=w2sb[:, q * 8 * 1024:(q + 1) * 8 * 1024].rearrange("p (t n) -> p t n", t=8),
                                    in_=w2src.rearrange("(t p) n -> p t n", p=128)[:, q * 8:(q + 1) * 8, :],
                                ).then_inc(s_w2.h, 16)
                        if e > 0:
                            emit_acc(e - 1, NCH - 1)
                        for c in range(NCH - 1):
                            emit_acc(e, c)
                    emit_acc(NE, NCH - 1)
                    g.wait_ge(s_ac.h, 16 * (NE + 1) * NCH * 4)

                @blk.sync
                def _(s):
                    for e in range(NE + 1):
                        if wbf is not None and e < NE:
                            s.wait_ge(s_pc.h, 16 * 2 * (e + 1))
                            if e > 0:
                                s.wait_ge(s_hm.h, e * NCH * 32)
                            s.dma_start(
                                out=w1sb.rearrange("p (t n) -> p t n", t=NT),
                                in_=wbf["W1"][e * D:(e + 1) * D, :].rearrange("(t p) n -> p t n", p=128),
                            ).then_inc(s_w1.h, 16)
                            if e > 0:
                                s.wait_ge(s_om.h, e * NCH * 4)
                            for q in range(2):
                                s.dma_start(
                                    out=w2sb[:, q * 16 * 1024:(q + 1) * 16 * 1024].rearrange("p (t n) -> p t n", t=16),
                                    in_=wbf["W2"][e * FF:(e + 1) * FF, :].rearrange("(t p) n -> p t n", p=128)[:, q * 16:(q + 1) * 16, :],
                                ).then_inc(s_w2.h, 16)
                        for c in range(NCH):
                            par = c % 2
                            idx = e * NCH + c
                            if idx >= 2:
                                s.wait_ge(s_hm.h, (idx - 1) * 32)
                            s.dma_start(
                                out=xcc[:, par * NT * CH:(par + 1) * NT * CH].rearrange("p (t n) -> p t n", t=NT),
                                in_=fmv(xbf_hbm, c * CH, (c + 1) * CH)).then_inc(s_xc.h, 16)
                            if e == NE:
                                for dq in range(4):
                                    s.wait_ge(s_ac.h, 16 * (((NE - 1) * NCH + c) * 4 + dq + 1))
                                    if c * 4 + dq >= 1:
                                        s.wait_ge(s_fl.h, (NE * NCH + c) * 4 + dq)
                                    s.dma_start(
                                        out=accb.rearrange("p (t n) -> p t n", t=2),
                                        in_=acc_hbm.rearrange("(t p) n -> p t n", p=128)[:, dq * 2:(dq + 1) * 2, c * CH:(c + 1) * CH],
                                    ).then_inc(s_av.h, 16)

                @blk.tensor
                def _(t):
                    nw1 = 16 if wbf is not None else 64
                    for e in range(NE + 1):
                        t.wait_ge(s_w1.h, nw1 * (e + 1))
                        for c in range(NCH):
                            par = c % 2
                            idx = e * NCH + c
                            ppc = idx % 2
                            t.wait_ge(s_xc.h, 16 * (idx + 1))
                            if idx >= 2:
                                t.wait_ge(s_cwv.h, idx - 1)
                            t.matmul(pcw[:, ppc * CH:(ppc + 1) * CH],
                                     cst["sel10"][:, e * 128:(e + 1) * 128],
                                     cwrp[0:10, c * CH:(c + 1) * CH],
                                     start=True, stop=True).then_inc(s_cwm.h, 1)
                            for f in range(32):
                                pg = f % 2
                                gidx = idx * 32 + f
                                if gidx >= 2:
                                    t.wait_ge(s_he.h, gidx - 1)
                                for k in range(NT):
                                    mm = t.matmul(ph[:, pg * CH:(pg + 1) * CH],
                                                  w1sb[:, k * FF + f * 128:k * FF + (f + 1) * 128],
                                                  xcc[:, (par * NT + k) * CH:(par * NT + k + 1) * CH],
                                                  start=(k == 0), stop=(k == NT - 1))
                                mm.then_inc(s_hm.h, 1)
                            t.wait_ge(s_he.h, 32 * (idx + 1))
                            if c == 0:
                                t.wait_ge(s_w2.h, 16 + (32 if wbf is not None else 64) * (e + 1))
                            for dq in range(4):
                                oidx = idx * 4 + dq
                                pq = (dq % 2) * 2
                                if oidx >= 2:
                                    t.wait_ge(s_fl.h, oidx - 1)
                                for m in range(2):
                                    dt_ = dq * 2 + m
                                    if e == 0:
                                        t.matmul(po[:, (pq + m) * CH:(pq + m + 1) * CH],
                                                 b2sb[:, dt_ * 128:(dt_ + 1) * 128],
                                                 cwrp[0:8, c * CH:(c + 1) * CH],
                                                 start=True, stop=False)
                                for k in range(32):
                                    for m in range(2):
                                        dt_ = dq * 2 + m
                                        mm = t.matmul(po[:, (pq + m) * CH:(pq + m + 1) * CH],
                                                      w2sb[:, k * 1024 + dt_ * 128:k * 1024 + (dt_ + 1) * 128],
                                                      hsb[:, k * CH:(k + 1) * CH],
                                                      start=(k == 0 and e > 0), stop=(k == 31))
                                mm.then_inc(s_om.h, 1)

                @blk.scalar
                def _(sc):
                    for e in range(NE + 1):
                        for c in range(NCH):
                            idx = e * NCH + c
                            for f in range(32):
                                pg = f % 2
                                bcol = (d1bcols[:, f:f + 1] if e == NE
                                        else b1cols[:, e * 32 + f:e * 32 + f + 1])
                                sc.wait_ge(s_hm.h, idx * 32 + f + 1)
                                sc.activation(out=hsb[:, f * CH:(f + 1) * CH],
                                              in_=ph[:, pg * CH:(pg + 1) * CH],
                                              func=AF.Relu, bias=bcol,
                                              scale=1.0).then_inc(s_he.h, 1)

                @blk.vector
                def _(v):
                    for e in range(NE + 1):
                        for c in range(NCH):
                            idx = e * NCH + c
                            ppc = idx % 2
                            v.wait_ge(s_cwm.h, idx + 1)
                            v.tensor_copy(cwx[:, :],
                                          pcw[:, ppc * CH:(ppc + 1) * CH]).then_inc(s_cwv.h, 1)
                            for dq in range(4):
                                oidx = idx * 4 + dq
                                pq = (dq % 2) * 2
                                v.wait_ge(s_om.h, oidx + 1)
                                if oidx >= 2:
                                    v.wait_ge(s_ac.h, 16 * (oidx - 1))
                                for m in range(2):
                                    tsl = tmp[:, ((oidx % 2) * 2 + m) * CH:((oidx % 2) * 2 + m + 1) * CH]
                                    if e == NE:
                                        dt_ = dq * 2 + m
                                        if m == 0:
                                            v.wait_ge(s_av.h, 16 * (c * 4 + dq + 1))
                                        v.tensor_scalar(out=u1[:, :], in0=po[:, (pq + m) * CH:(pq + m + 1) * CH],
                                                        scalar1=d2bcols[:, dt_:dt_ + 1], scalar2=None, op0=OP.add)
                                        v.tensor_tensor(tsl, accb[:, m * CH:(m + 1) * CH],
                                                        u1[:, :], OP.subtract)
                                        v.tensor_tensor(tsl, tsl, cwx[:, :], OP.mult)
                                        tt = v.tensor_tensor(tsl, tsl, u1[:, :], OP.add)
                                    else:
                                        tt = v.tensor_tensor(
                                            tsl,
                                            po[:, (pq + m) * CH:(pq + m + 1) * CH],
                                            cwx[:, :], OP.mult)
                                tt.then_inc(s_fl.h, 1)



def emit_l1attn(nc, cst, xbf_hbm, xfm_hbm, wts, y_hbm, reads_hbm, scr_hbm, ln_out=None):
    with contextlib.ExitStack() as ctx:
        sb = ctx.enter_context
        cols = sb(nc.sbuf_tensor("a_cols", [128, 8], F32))
        mtb = sb(nc.sbuf_tensor("a_mt", [128, C], BF16))
        sm1 = Sem(ctx, nc, "a_m1")
        ctxAB = contextlib.ExitStack()
        ctxAB.__enter__()
        xb = ctxAB.enter_context(nc.sbuf_tensor("a_xb", [128, NT * TC], BF16))

        with contextlib.ExitStack() as c0:
            tin = c0.enter_context(nc.sbuf_tensor("a_tin", [128, C], F32))
            trow = c0.enter_context(nc.sbuf_tensor("a_trow", [1, 8], F32))
            pb = c0.enter_context(nc.psum_tensor("a_pb", [128, 16], F32))
            s_l = Sem(c0, nc, "a_l")
            s_v = Sem(c0, nc, "a_v")
            s_s = Sem(c0, nc, "a_s")
            s_t = Sem(c0, nc, "a_t")
            with nc.Block() as blk:
                @blk.sync
                def _(s):
                    for q in range(4):
                        s.dma_start(
                            out=xb[:, q * 2 * TC:(q + 1) * 2 * TC].rearrange("p (t n) -> p t n", t=2),
                            in_=xbf_hbm.rearrange("(t p) n -> p t n", p=128)[:, q * 2:(q + 1) * 2, :],
                        ).then_inc(s_l.h, 16)

                @blk.gpsimd
                def _(g):
                    g.dma_start(out=trow[:, 0:1], in_=wts["decay"][:, :]).then_inc(s_l.h, 16)
                    g.dma_start(out=trow[:, 1:2], in_=wts["log_alpha"][:, :]).then_inc(s_l.h, 16)

                @blk.scalar
                def _(sc):
                    sc.wait_ge(s_l.h, 96)
                    sc.activation(out=trow[:, 2:3], in_=trow[:, 0:1], func=AF.Sigmoid)
                    sc.drain()
                    sc.activation(out=trow[:, 3:4], in_=trow[:, 2:3], func=AF.Ln)
                    sc.drain()
                    sc.activation(out=trow[:, 4:5], in_=trow[:, 1:2], func=AF.Exp)
                    sc.drain()
                    sc.activation(out=trow[:, 5:6], in_=trow[:, 3:4], func=AF.Exp,
                                  scale=64.0).then_inc(s_s.h, 1)
                    sc.wait_ge(s_v.h, 2)
                    sc.activation(out=tin[:, :], in_=tin[:, :], func=AF.Exp)
                    sc.drain()
                    sc.activation(out=cols[:, 1:2], in_=cols[:, 1:2], func=AF.Exp)
                    sc.drain()
                    sc.activation(out=cols[:, 2:3], in_=cols[:, 2:3], func=AF.Exp).then_inc(s_s.h, 1)

                @blk.tensor
                def _(t):
                    t.wait_ge(s_s.h, 1)
                    t.matmul(pb[:, 0:8], cst["onesrf"][:, :], trow[:, :],
                             start=True, stop=True).then_inc(s_t.h, 1)

                @blk.vector
                def _(v):
                    v.wait_ge(s_t.h, 1)
                    v.tensor_scalar(out=tin[:, :], in0=cst["dm1T2"][:, :], scalar1=pb[:, 3:4],
                                    scalar2=None, op0=OP.mult).then_inc(s_v.h, 1)
                    v.tensor_scalar(out=cols[:, 1:2], in0=cst["iota2_c"][:, :], scalar1=pb[:, 3:4],
                                    scalar2=None, op0=OP.mult)
                    v.drain()
                    v.tensor_scalar(out=cols[:, 2:3], in0=cst["iota2_cm1"][:, :], scalar1=pb[:, 3:4],
                                    scalar2=None, op0=OP.mult)
                    v.drain()
                    v.tensor_copy(cols[:, 3:4], pb[:, 5:6])
                    v.tensor_copy(cols[:, 4:5], pb[:, 4:5]).then_inc(s_v.h, 2)
                    v.wait_ge(s_s.h, 2)
                    v.drain()
                    v.tensor_tensor(mtb[:, :], tin[:, :], cst["striu2"][:, :], OP.mult).then_inc(sm1.h, 1)

        vtm = ctxAB.enter_context(nc.sbuf_tensor("a_vtm", [128, 32 * D], BF16))
        with contextlib.ExitStack() as cA:
            wwT = cA.enter_context(nc.sbuf_tensor("a_wwT", [128, NT * D], BF16))
            wn = cA.enter_context(nc.sbuf_tensor("a_wn", [128, NT * D], BF16))
            pv = cA.enter_context(nc.psum_tensor("a_pv", [128, 2 * CH], F32))
            s_wl = Sem(cA, nc, "a_wl")
            s_sc = Sem(cA, nc, "a_sc")
            s_tr = Sem(cA, nc, "a_tr")
            s_mm = Sem(cA, nc, "a_mm")
            s_ev = Sem(cA, nc, "a_ev")
            with nc.Block() as blk:
                @blk.gpsimd
                def _(g):
                    g.dma_start(out=wn.rearrange("p (t n) -> p t n", t=NT),
                                in_=wts["W_write"].rearrange("(t p) n -> p t n", p=128)).then_inc(s_wl.h, 16)

                @blk.sync
                def _(s):
                    s.wait_ge(s_wl.h, 16)
                    s.dma_start(out=scr_hbm.rearrange("(t p) n -> p t n", p=128),
                                in_=wn.rearrange("p (t n) -> p t n", t=NT)).then_inc(s_sc.h, 16)
                    s.wait_ge(s_sc.h, 16)
                    for i in range(NT):
                        s.dma_start_transpose(wwT[:, i * D:(i + 1) * D],
                                              scr_hbm[:, i * 128:(i + 1) * 128]).then_inc(s_tr.h, 16)

                @blk.tensor
                def _(t):
                    t.wait_ge(s_tr.h, 16 * NT)
                    for gi in range(64):
                        ti, hf = gi // 2, gi % 2
                        if gi >= 2:
                            t.wait_ge(s_ev.h, gi - 1)
                        for k in range(NT):
                            mm = t.matmul(pv[:, (gi % 2) * CH:(gi % 2 + 1) * CH],
                                          xb[:, k * TC + ti * 128:k * TC + (ti + 1) * 128],
                                          wwT[:, k * D + hf * CH:k * D + (hf + 1) * CH],
                                          start=(k == 0), stop=(k == NT - 1))
                        mm.then_inc(s_mm.h, 1)

                @blk.scalar
                def _(scl):
                    for gi in range(64):
                        scl.wait_ge(s_mm.h, gi + 1)
                        ti, hf = gi // 2, gi % 2
                        scl.copy(vtm[:, ti * D + hf * CH:ti * D + (hf + 1) * CH],
                                 pv[:, (gi % 2) * CH:(gi % 2 + 1) * CH]).then_inc(s_ev.h, 1)

        wst = ctxAB.enter_context(nc.sbuf_tensor("a_wst", [128, 2 * NT * D], BF16))
        with contextlib.ExitStack() as cB:
            wtm = cB.enter_context(nc.sbuf_tensor("a_wtm", [128, 2 * D], BF16))
            vvg = cB.enter_context(nc.sbuf_tensor("a_vvg", [128, 2 * D], BF16))
            smt = cB.enter_context(nc.sbuf_tensor("a_smt", [128, 2 * C], BF16))
            rsi = cB.enter_context(nc.sbuf_tensor("a_rsi", [128, 2 * D], BF16))
            rsb = cB.enter_context(nc.sbuf_tensor("a_rsb", [128, 2 * D], BF16))
            w0s = cB.enter_context(nc.sbuf_tensor("a_w0", [128, 2 * NT * C], BF16))
            PBb = cB.enter_context(nc.psum_tensor("a_PBb", [128, NT * 128], BF16))
            PST = cB.enter_context(nc.psum_tensor("a_PST", [128, C], F32))
            PRD = cB.enter_context(nc.psum_tensor("a_PRD", [128, D], F32))
            PDA = cB.enter_context(nc.psum_tensor("a_PDA", [128, D], F32))
            PDB = cB.enter_context(nc.psum_tensor("a_PDB", [128, D], F32))

            s_z = Sem(cB, nc, "a_z")
            s_tw = Sem(cB, nc, "a_tw")
            s_wtm = Sem(cB, nc, "a_wtm")
            s_st = Sem(cB, nc, "a_st")
            s_smt = Sem(cB, nc, "a_smt")
            s_in = Sem(cB, nc, "a_in")
            s_ie = Sem(cB, nc, "a_ie")
            s_rd = Sem(cB, nc, "a_rd")
            s_rev = Sem(cB, nc, "a_rev")
            s_rst = Sem(cB, nc, "a_rst")
            s_vvg = Sem(cB, nc, "a_vvg")
            s_dm = Sem(cB, nc, "a_dm")
            s_mg = Sem(cB, nc, "a_mg")

            with nc.Block() as blk:
                @blk.vector
                def _(v):
                    v.memset(wst[:, :], 0.0)
                    v.memset(w0s[:, :], 0.0)
                    v.drain()
                    for b in range(2):
                        for k in range(NT):
                            cpz = v.tensor_copy(w0s[:, (b * NT + k) * C + 1:(b * NT + k + 1) * C],
                                                xb[:, k * TC + b * SEQ:k * TC + b * SEQ + C - 1])
                    cpz.then_inc(s_z.h, 1)
                    for i in range(64):
                        b, n = i % 2, i // 2
                        T0 = b * SEQ + n * C
                        par = i % 2
                        ti, cb = T0 // 128, 64 * (n % 2)
                        v.wait_ge(s_st.h, i + 1)
                        v.tensor_tensor(smt[cb:cb + 64, par * C:(par + 1) * C], PST[cb:cb + 64, :],
                                        mtb[cb:cb + 64, :], OP.mult).then_inc(s_smt.h, 1)
                        v.tensor_scalar(out=vvg[cb:cb + 64, par * D:(par + 1) * D],
                                        in0=vtm[cb:cb + 64, ti * D:(ti + 1) * D],
                                        scalar1=cols[cb:cb + 64, 2:3], scalar2=None,
                                        op0=OP.mult).then_inc(s_vvg.h, 1)
                        v.wait_ge(s_ie.h, i + 1)
                        v.wait_ge(s_rd.h, i + 1)
                        if i >= 2:
                            v.wait_ge(s_rst.h, 16 * (i - 1))
                        v.scalar_tensor_tensor(out=rsb[cb:cb + 64, par * D:(par + 1) * D],
                                               in0=PRD[cb:cb + 64, :],
                                               scalar=cols[cb:cb + 64, 1:2],
                                               in1=rsi[cb:cb + 64, par * D:(par + 1) * D],
                                               op0=OP.mult, op1=OP.add).then_inc(s_rev.h, 1)
                        for et in range(8):
                            v.wait_ge(s_dm.h, 8 * i + et + 1)
                            ps = PDA if et % 2 == 0 else PDB
                            v.scalar_tensor_tensor(
                                out=wst[:, (b * NT + et) * D:(b * NT + et + 1) * D],
                                in0=wst[:, (b * NT + et) * D:(b * NT + et + 1) * D],
                                scalar=cols[:, 3:4], in1=ps[:, :],
                                op0=OP.mult, op1=OP.add).then_inc(s_mg.h, 1)

                @blk.scalar
                def _(scl):
                    for i in range(64):
                        b, n = i % 2, i // 2
                        par = i % 2
                        cb = 64 * (n % 2)
                        scl.wait_ge(s_tw.h, i + 1)
                        if i >= 2:
                            scl.wait_ge(s_dm.h, 8 * (i - 1))
                        for k in range(NT):
                            cpw = scl.copy(wtm[cb:cb + 64, par * D + k * 128:par * D + (k + 1) * 128],
                                           PBb[cb:cb + 64, k * 128:(k + 1) * 128])
                        cpw.then_inc(s_wtm.h, 1)
                        scl.wait_ge(s_in.h, i + 1)
                        if i >= 2:
                            scl.wait_ge(s_rev.h, i - 1)
                        scl.copy(rsi[cb:cb + 64, par * D:(par + 1) * D],
                                 PRD[cb:cb + 64, :]).then_inc(s_ie.h, 1)

                @blk.tensor
                def _(t):
                    t.wait_ge(s_z.h, 1)
                    t.wait_ge(sm1.h, 1)
                    for i in range(64):
                        b, n = i % 2, i // 2
                        T0 = b * SEQ + n * C
                        par = i % 2
                        ti, cb = T0 // 128, 64 * (n % 2)
                        t.wait_ge(s_wtm.h, i)
                        for k in range(NT):
                            if n == 0:
                                src = w0s[:, (b * NT + k) * C:(b * NT + k + 1) * C]
                            else:
                                src = xb[:, k * TC + T0 - 1:k * TC + T0 + C - 1]
                            tp = t.transpose(PBb[cb:cb + 64, k * 128:(k + 1) * 128], src,
                                             cst["identb"][:, :])
                        tp.then_inc(s_tw.h, 1)
                        t.wait_ge(s_smt.h, i)
                        for k in range(NT):
                            wsrc = (w0s[:, (b * NT + k) * C:(b * NT + k + 1) * C] if n == 0
                                    else xb[:, k * TC + T0 - 1:k * TC + T0 + C - 1])
                            mm = t.matmul(PST[cb:cb + 64, :], wsrc,
                                          xb[:, k * TC + T0:k * TC + T0 + C],
                                          start=(k == 0), stop=(k == NT - 1))
                        mm.then_inc(s_st.h, 1)
                        t.wait_ge(s_smt.h, i + 1)
                        t.wait_ge(s_rev.h, i)
                        for hf in range(2):
                            mm = t.matmul(PRD[cb:cb + 64, hf * CH:(hf + 1) * CH],
                                          smt[cb:cb + 64, par * C:(par + 1) * C],
                                          vtm[cb:cb + 64, ti * D + hf * CH:ti * D + (hf + 1) * CH],
                                          start=True, stop=True)
                        mm.then_inc(s_in.h, 1)
                        t.wait_ge(s_ie.h, i + 1)
                        if i >= 2:
                            t.wait_ge(s_mg.h, 8 * (i - 1))
                        for k in range(NT):
                            for hf in range(2):
                                mm = t.matmul(PRD[cb:cb + 64, hf * CH:(hf + 1) * CH],
                                              xb[:, k * TC + T0:k * TC + T0 + C],
                                              wst[:, (b * NT + k) * D + hf * CH:(b * NT + k) * D + (hf + 1) * CH],
                                              start=(k == 0), stop=(k == NT - 1))
                        mm.then_inc(s_rd.h, 1)
                        t.wait_ge(s_vvg.h, i + 1)
                        t.wait_ge(s_wtm.h, i + 1)
                        for et in range(8):
                            if 8 * i + et >= 2:
                                t.wait_ge(s_mg.h, 8 * i + et - 1)
                            ps = PDA if et % 2 == 0 else PDB
                            for hf in range(2):
                                mm = t.matmul(ps[:, hf * CH:(hf + 1) * CH],
                                              wtm[cb:cb + 64, par * D + et * 128:par * D + (et + 1) * 128],
                                              vvg[cb:cb + 64, par * D + hf * CH:par * D + (hf + 1) * CH],
                                              start=True, stop=True)
                            mm.then_inc(s_dm.h, 1)

                @blk.gpsimd
                def _(g):
                    for i in range(64):
                        b, n = i % 2, i // 2
                        T0 = b * SEQ + n * C
                        par = i % 2
                        cb = 64 * ((i // 2) % 2)
                        g.wait_ge(s_rev.h, i + 1)
                        g.dma_start(out=reads_hbm[T0:T0 + C, :],
                                    in_=rsb[cb:cb + 64, par * D:(par + 1) * D]).then_inc(s_rst.h, 16)
                    g.wait_ge(s_rst.h, 16 * 64)

        ctxAB.__exit__(None, None, None)
        if ln_out is None:
            with contextlib.ExitStack() as cC:
                rfm = cC.enter_context(nc.sbuf_tensor("a_rfm", [128, 2 * NT * CH], BF16))
                wrT = cC.enter_context(nc.sbuf_tensor("a_wrT", [128, NT * D], BF16))
                x2c = cC.enter_context(nc.sbuf_tensor("a_x2c", [128, 2 * NT * CH], F32))
                ysb = cC.enter_context(nc.sbuf_tensor("a_ysb", [128, NT * CH], F32))
                pj = cC.enter_context(nc.psum_tensor("a_pj", [128, 4 * CH], F32))
                s_wl = Sem(cC, nc, "c_wl")
                s_sc = Sem(cC, nc, "c_sc")
                s_tr = Sem(cC, nc, "c_tr")
                s_x = Sem(cC, nc, "c_x")
                s_mm = Sem(cC, nc, "c_mm")
                s_ev = Sem(cC, nc, "c_ev")
                s_y = Sem(cC, nc, "c_y")
                with nc.Block() as blk:
                    @blk.gpsimd
                    def _(g):
                        g.dma_start(out=wn2.rearrange("p (t n) -> p t n", t=NT),
                                    in_=wts["W_read"].rearrange("(t p) n -> p t n", p=128)).then_inc(s_wl.h, 16)
                        for c in range(NCH):
                            g.wait_ge(s_ev.h, NT * (c + 1))
                            g.dma_start(out=fmv(y_hbm, c * CH, (c + 1) * CH),
                                        in_=ysb.rearrange("p (t n) -> p t n", t=NT)).then_inc(s_y.h, 16)
                        g.wait_ge(s_y.h, 16 * NCH)

                    @blk.sync
                    def _(s):
                        s.wait_ge(s_wl.h, 16)
                        s.dma_start(out=scr_hbm.rearrange("(t p) n -> p t n", p=128),
                                    in_=wn2.rearrange("p (t n) -> p t n", t=NT)).then_inc(s_sc.h, 16)
                        s.wait_ge(s_sc.h, 16)
                        for i in range(NT):
                            s.dma_start_transpose(wrT[:, i * D:(i + 1) * D],
                                                  scr_hbm[:, i * 128:(i + 1) * 128]).then_inc(s_tr.h, 16)
                        for i in range(NT):
                            s.dma_start_transpose(rfm[:, i * TC:(i + 1) * TC],
                                                  reads_hbm[:, i * 128:(i + 1) * 128]).then_inc(s_tr.h, 16)
                        for c in range(NCH):
                            par = c % 2
                            if c >= 2:
                                s.wait_ge(s_ev.h, NT * (c - 1))
                            s.dma_start(
                                out=x2c[:, par * NT * CH:(par + 1) * NT * CH].rearrange("p (t n) -> p t n", t=NT),
                                in_=fmv(xfm_hbm, c * CH, (c + 1) * CH)).then_inc(s_x.h, 16)

                    @blk.tensor
                    def _(t):
                        t.wait_ge(s_tr.h, 16 * NT)
                        for c in range(NCH):
                            t.wait_ge(s_tr.h, 16 * NT * (c + 2))
                            for m in range(NT):
                                gi = c * NT + m
                                pq = gi % 4
                                if gi >= 4:
                                    t.wait_ge(s_ev.h, gi - 3)
                                for k in range(NT):
                                    mm = t.matmul(pj[:, pq * CH:(pq + 1) * CH],
                                                  wrT[:, k * D + m * 128:k * D + (m + 1) * 128],
                                                  rfm[:, (c % 2) * NT * CH + k * CH:(c % 2) * NT * CH + (k + 1) * CH],
                                                  start=(k == 0), stop=(k == NT - 1))
                                mm.then_inc(s_mm.h, 1)

                    @blk.vector
                    def _(v):
                        for c in range(NCH):
                            par = c % 2
                            v.wait_ge(s_x.h, 16 * (c + 1))
                            for m in range(NT):
                                gi = c * NT + m
                                pq = gi % 4
                                v.wait_ge(s_mm.h, gi + 1)
                                if c > 0 and m == 0:
                                    v.wait_ge(s_y.h, 16 * c)
                                v.scalar_tensor_tensor(
                                    out=ysb[:, m * CH:(m + 1) * CH], in0=pj[:, pq * CH:(pq + 1) * CH],
                                    scalar=cols[:, 4:5],
                                    in1=x2c[:, (par * NT + m) * CH:(par * NT + m + 1) * CH],
                                    op0=OP.mult, op1=OP.add).then_inc(s_ev.h, 1)
        else:
            x3_hbm, x3bf_hbm = ln_out
            with contextlib.ExitStack() as cC:
                rfm = cC.enter_context(nc.sbuf_tensor("a_rfm", [128, 2 * NT * CH], BF16))
                wrT = cC.enter_context(nc.sbuf_tensor("a_wrT", [128, NT * D], BF16))
                x2c = cC.enter_context(nc.sbuf_tensor("a_x2c", [128, 2 * NT * CH], F32))
                ysb = cC.enter_context(nc.sbuf_tensor("a_ysb", [128, NT * CH], F32))
                tb = cC.enter_context(nc.sbuf_tensor("a_tb", [128, 2 * NT * CH], F32))
                tbf = cC.enter_context(nc.sbuf_tensor("a_tbf", [128, NT * CH], BF16))
                sqb = cC.enter_context(nc.sbuf_tensor("a_sqb", [128, NT * CH], BF16))
                xob = cC.enter_context(nc.sbuf_tensor("a_xob", [128, NT * CH], F32))
                xbb = cC.enter_context(nc.sbuf_tensor("a_xbb", [128, NT * CH], BF16))
                srow = cC.enter_context(nc.sbuf_tensor("a_srow", [1, 6 * CH], F32))
                epst = cC.enter_context(nc.sbuf_tensor("a_eps", [1, 1], F32))
                pj = cC.enter_context(nc.psum_tensor("a_pj", [128, 4 * CH], F32))
                ps_st = cC.enter_context(nc.psum_tensor("a_pst", [64, CH], F32))
                ps_bc = cC.enter_context(nc.psum_tensor("a_pbc", [128, 2 * CH], F32))
                s_wl = Sem(cC, nc, "c_wl")
                s_sc = Sem(cC, nc, "c_sc")
                s_tr = Sem(cC, nc, "c_tr")
                s_x = Sem(cC, nc, "c_x")
                s_mm = Sem(cC, nc, "c_mm")
                s_ev = Sem(cC, nc, "c_ev")
                s_t = Sem(cC, nc, "c_t")
                s_tbf = Sem(cC, nc, "c_tbf")
                s_sq = Sem(cC, nc, "c_sq")
                s_stats = Sem(cC, nc, "c_st")
                s_var = Sem(cC, nc, "c_var")
                s_rstd = Sem(cC, nc, "c_rs")
                s_bc = Sem(cC, nc, "c_bc")
                s_ap = Sem(cC, nc, "c_ap")
                s_out = Sem(cC, nc, "c_out")
                with nc.Block() as blk:
                    @blk.gpsimd
                    def _(g):
                        g.dma_start(out=scr_hbm[:, :], in_=wts["W_read"][:, :]).then_inc(s_sc.h, 16)
                        for c in range(NCH):
                            g.wait_ge(s_ap.h, c + 1)
                            g.dma_start(out=fmv(x3_hbm, c * CH, (c + 1) * CH),
                                        in_=xob.rearrange("p (t n) -> p t n", t=NT)).then_inc(s_out.h, 16)
                            g.dma_start(out=fmv(x3bf_hbm, c * CH, (c + 1) * CH),
                                        in_=xbb.rearrange("p (t n) -> p t n", t=NT)).then_inc(s_out.h, 16)
                        g.wait_ge(s_out.h, 32 * NCH)

                    @blk.sync
                    def _(s):
                        s.wait_ge(s_sc.h, 16)
                        for i in range(NT):
                            s.dma_start_transpose(wrT[:, i * D:(i + 1) * D],
                                                  scr_hbm[:, i * 128:(i + 1) * 128]).then_inc(s_tr.h, 16)
                        for c in range(NCH):
                            par = c % 2
                            if c >= 2:
                                s.wait_ge(s_mm.h, (c - 1) * NT)
                            for k in range(NT):
                                s.dma_start_transpose(
                                    rfm[:, par * NT * CH + k * CH:par * NT * CH + (k + 1) * CH],
                                    reads_hbm[c * CH:(c + 1) * CH, k * 128:(k + 1) * 128]).then_inc(s_tr.h, 16)
                            if c >= 2:
                                s.wait_ge(s_t.h, c - 1)
                            s.dma_start(
                                out=x2c[:, par * NT * CH:(par + 1) * NT * CH].rearrange("p (t n) -> p t n", t=NT),
                                in_=fmv(xfm_hbm, c * CH, (c + 1) * CH)).then_inc(s_x.h, 16)

                    @blk.tensor
                    def _(t):
                        def ln_stats(cc):
                            t.wait_ge(s_tbf.h, cc + 1)
                            t.wait_ge(s_sq.h, cc + 1)
                            if cc > 0:
                                t.wait_ge(s_var.h, cc)
                            for i in range(NT):
                                t.matmul(ps_st[0:1, :], cst["onesc"][:, :], tbf[:, i * CH:(i + 1) * CH],
                                         start=(i == 0), stop=(i == NT - 1))
                            for i in range(NT):
                                mm = t.matmul(ps_st[32:33, :], cst["onesc"][:, :], sqb[:, i * CH:(i + 1) * CH],
                                              start=(i == 0), stop=(i == NT - 1))
                            mm.then_inc(s_stats.h, 1)

                        def ln_bcast(cc):
                            t.wait_ge(s_rstd.h, cc + 1)
                            if cc > 0:
                                t.wait_ge(s_ap.h, cc)
                            t.matmul(ps_bc[:, 0:CH], cst["onesrf"][:, :], srow[:, 0:CH],
                                     start=True, stop=True)
                            t.matmul(ps_bc[:, CH:2 * CH], cst["onesrf"][:, :], srow[:, 4 * CH:5 * CH],
                                     start=True, stop=True).then_inc(s_bc.h, 1)

                        t.wait_ge(s_tr.h, 16 * NT)
                        for c in range(NCH):
                            t.wait_ge(s_tr.h, 16 * NT * (c + 2))
                            for m in range(NT):
                                gi = c * NT + m
                                pq = gi % 4
                                if gi >= 4:
                                    t.wait_ge(s_ev.h, gi - 3)
                                for k in range(NT):
                                    mm = t.matmul(pj[:, pq * CH:(pq + 1) * CH],
                                                  wrT[:, k * D + m * 128:k * D + (m + 1) * 128],
                                                  rfm[:, (c % 2) * NT * CH + k * CH:(c % 2) * NT * CH + (k + 1) * CH],
                                                  start=(k == 0), stop=(k == NT - 1))
                                mm.then_inc(s_mm.h, 1)
                            if c >= 1:
                                ln_stats(c - 1)
                            if c >= 2:
                                ln_bcast(c - 2)
                        ln_stats(NCH - 1)
                        ln_bcast(NCH - 2)
                        ln_bcast(NCH - 1)

                    @blk.scalar
                    def _(scl):
                        for cc in range(NCH):
                            scl.wait_ge(s_t.h, cc + 1)
                            if cc > 0:
                                scl.wait_ge(s_stats.h, cc)
                            ts_ = (cc % 2) * NT * CH
                            scl.activation(out=sqb[:, :], in_=tb[:, ts_:ts_ + NT * CH],
                                           func=AF.Square).then_inc(s_sq.h, 1)
                            scl.wait_ge(s_var.h, cc + 1)
                            scl.activation(out=srow[:, 3 * CH:4 * CH], in_=srow[:, CH:2 * CH],
                                           func=AF.Ln, bias=epst[0:1, 0:1])
                            scl.activation(out=srow[:, 4 * CH:5 * CH], in_=srow[:, 3 * CH:4 * CH],
                                           func=AF.Exp, scale=-0.5).then_inc(s_rstd.h, 1)

                    @blk.vector
                    def _(v):
                        def ln_row(cc):
                            v.wait_ge(s_stats.h, cc + 1)
                            if cc > 0:
                                v.wait_ge(s_bc.h, cc)
                            v.tensor_scalar(out=srow[:, 0:CH], in0=ps_st[0:1, :], scalar1=1.0 / D,
                                            scalar2=None, op0=OP.mult)
                            v.drain()
                            v.tensor_tensor(srow[:, 2 * CH:3 * CH], srow[:, 0:CH], srow[:, 0:CH], OP.mult)
                            v.drain()
                            v.scalar_tensor_tensor(out=srow[:, CH:2 * CH], in0=ps_st[32:33, :],
                                                   scalar=1.0 / D, in1=srow[:, 2 * CH:3 * CH],
                                                   op0=OP.mult, op1=OP.subtract).then_inc(s_var.h, 1)

                        def ln_apply(cc):
                            v.wait_ge(s_bc.h, cc + 1)
                            if cc > 0:
                                v.wait_ge(s_out.h, 32 * cc)
                            ts_ = (cc % 2) * NT * CH
                            for i in range(NT):
                                v.tensor_tensor(xob[:, i * CH:(i + 1) * CH],
                                                tb[:, ts_ + i * CH:ts_ + (i + 1) * CH],
                                                ps_bc[:, 0:CH], OP.subtract)
                            for i in range(NT):
                                v.tensor_tensor(xob[:, i * CH:(i + 1) * CH], xob[:, i * CH:(i + 1) * CH],
                                                ps_bc[:, CH:2 * CH], OP.mult)
                            v.tensor_copy(xbb[:, :], xob[:, :]).then_inc(s_ap.h, 1)

                        v.memset(epst[:, :], 1e-5)
                        for c in range(NCH):
                            par = c % 2
                            v.wait_ge(s_x.h, 16 * (c + 1))
                            for m in range(NT):
                                gi = c * NT + m
                                pq = gi % 4
                                v.wait_ge(s_mm.h, gi + 1)
                                v.scalar_tensor_tensor(
                                    out=ysb[:, m * CH:(m + 1) * CH], in0=pj[:, pq * CH:(pq + 1) * CH],
                                    scalar=cols[:, 4:5],
                                    in1=x2c[:, (par * NT + m) * CH:(par * NT + m + 1) * CH],
                                    op0=OP.mult, op1=OP.add).then_inc(s_ev.h, 1)
                            if c >= 2:
                                ln_apply(c - 2)
                            if c >= 2:
                                v.wait_ge(s_sq.h, c - 1)
                            if c >= 1:
                                v.wait_ge(s_stats.h, c)
                            ts_ = (c % 2) * NT * CH
                            for m in range(NT):
                                tt = v.tensor_tensor(tb[:, ts_ + m * CH:ts_ + (m + 1) * CH],
                                                     ysb[:, m * CH:(m + 1) * CH],
                                                     x2c[:, (par * NT + m) * CH:(par * NT + m + 1) * CH],
                                                     OP.add)
                            tt.then_inc(s_t.h, 1)
                            v.tensor_copy(tbf[:, :], tb[:, ts_:ts_ + NT * CH]).then_inc(s_tbf.h, 1)
                            if c >= 1:
                                ln_row(c - 1)
                        ln_row(NCH - 1)
                        ln_apply(NCH - 2)
                        ln_apply(NCH - 1)


def emit_l0attn(nc, cst, xbf_hbm, xfm_hbm, wts, y_hbm, scr_hbm, precast=None, s_pc=None,
                bz=None, ln_out=None):
    NSC = 8
    with contextlib.ExitStack() as ctx:
        sb = ctx.enter_context
        ofm = sb(nc.sbuf_tensor("d_ofm", [128, NT * TC], BF16))
        hcols = sb(nc.sbuf_tensor("d_hc", [128, 4 * 8], F32))
        masks = sb(nc.sbuf_tensor("d_msk", [128, 3 * 8 * C], BF16))
        dwbc = sb(nc.sbuf_tensor("d_dwb", [128, 8 * C], BF16))
        s_cst = Sem(ctx, nc, "d_cst")

        with contextlib.ExitStack() as c0:
            trow = c0.enter_context(nc.sbuf_tensor("d_tr", [1, 8], F32))
            tmpm = c0.enter_context(nc.sbuf_tensor("d_tm", [128, C], F32))
            pb = c0.enter_context(nc.psum_tensor("d_pb", [128, 16], F32))
            s_l = Sem(c0, nc, "d_l")
            s_t = Sem(c0, nc, "d_t")
            s_v = Sem(c0, nc, "d_v")
            s_s = Sem(c0, nc, "d_s")
            with nc.Block() as blk:
                @blk.gpsimd
                def _(g):
                    g.dma_start(out=trow[:, :], in_=wts["decay"].rearrange("(o e) -> o e", o=1)).then_inc(s_l.h, 16)

                @blk.scalar
                def _(sc):
                    sc.wait_ge(s_l.h, 16)
                    sc.activation(out=trow[:, :], in_=trow[:, :], func=AF.Sigmoid)
                    sc.drain()
                    sc.activation(out=trow[:, :], in_=trow[:, :], func=AF.Ln).then_inc(s_s.h, 1)
                    sc.wait_ge(s_v.h, 1)
                    sc.activation(out=hcols[:, 0:8], in_=hcols[:, 0:8], func=AF.Exp)
                    sc.drain()
                    sc.activation(out=hcols[:, 8:16], in_=hcols[:, 8:16], func=AF.Exp)
                    sc.drain()
                    sc.activation(out=hcols[:, 16:24], in_=hcols[:, 16:24], func=AF.Exp).then_inc(s_s.h, 1)
                    for h in range(8):
                        sc.wait_ge(s_v.h, 2 + 3 * h)
                        sc.activation(out=tmpm[:, :], in_=tmpm[:, :], func=AF.Exp).then_inc(s_s.h, 1)
                        sc.wait_ge(s_v.h, 3 + 3 * h)
                        sc.activation(out=tmpm[:, :], in_=tmpm[:, :], func=AF.Exp).then_inc(s_s.h, 1)
                        sc.wait_ge(s_v.h, 4 + 3 * h)
                        sc.activation(out=tmpm[:, :], in_=tmpm[:, :], func=AF.Exp).then_inc(s_s.h, 1)

                @blk.tensor
                def _(t):
                    t.wait_ge(s_s.h, 1)
                    t.matmul(pb[:, 0:8], cst["onesrf"][:, :], trow[:, :], start=True, stop=True).then_inc(s_t.h, 1)

                @blk.vector
                def _(v):
                    v.wait_ge(s_t.h, 1)
                    v.tensor_copy(hcols[:, 24:32], pb[:, 0:8])
                    v.drain()
                    v.tensor_scalar(out=hcols[:, 0:8], in0=hcols[:, 24:32], scalar1=cst["iota2_p1"][:, :],
                                    scalar2=None, op0=OP.mult)
                    v.tensor_scalar(out=hcols[:, 8:16], in0=hcols[:, 24:32], scalar1=cst["iota2_cm1"][:, :],
                                    scalar2=None, op0=OP.mult)
                    v.tensor_scalar(out=hcols[:, 16:24], in0=hcols[:, 24:32], scalar1=64.0,
                                    scalar2=None, op0=OP.mult).then_inc(s_v.h, 1)
                    for h in range(8):
                        v.wait_ge(s_s.h, 2 + 3 * h)
                        v.tensor_scalar(out=tmpm[:, :], in0=cst["dmat2"][:, :],
                                        scalar1=hcols[:, 24 + h:25 + h], scalar2=None,
                                        op0=OP.mult).then_inc(s_v.h, 1)
                        v.wait_ge(s_s.h, 2 + 3 * h + 1)
                        v.tensor_tensor(masks[:, h * C:(h + 1) * C], tmpm[:, :],
                                        cst["stril2"][:, :], OP.mult)
                        v.tensor_scalar(out=tmpm[:, :], in0=cst["dmat2T"][:, :],
                                        scalar1=hcols[:, 24 + h:25 + h], scalar2=None,
                                        op0=OP.mult).then_inc(s_v.h, 1)
                        v.wait_ge(s_s.h, 2 + 3 * h + 2)
                        v.tensor_tensor(masks[:, (8 + h) * C:(9 + h) * C], tmpm[:, :],
                                        cst["triu2v"][:, :], OP.mult)
                        v.tensor_scalar(out=tmpm[:, :], in0=cst["rowc1"][:, :],
                                        scalar1=hcols[:, 24 + h:25 + h], scalar2=None,
                                        op0=OP.mult).then_inc(s_v.h, 1)
                        v.wait_ge(s_s.h, 2 + 3 * h + 3)
                        v.tensor_copy(masks[:, (16 + h) * C:(17 + h) * C], tmpm[:, :])
                    v.tensor_copy(hcols[0:1, 0:1], hcols[0:1, 0:1]).then_inc(s_cst.h, 1)

        with contextlib.ExitStack() as c0b:
            tmpm = c0b.enter_context(nc.sbuf_tensor("d_tm2", [128, C], F32))
            s_v = Sem(c0b, nc, "d_v2")
            s_s = Sem(c0b, nc, "d_s2")
            with nc.Block() as blk:
                @blk.vector
                def _(v):
                    for h in range(8):
                        if h > 0:
                            v.wait_ge(s_s.h, h)
                        v.tensor_scalar(out=tmpm[:, :], in0=cst["rowcm1"][:, :],
                                        scalar1=hcols[:, 24 + h:25 + h], scalar2=None,
                                        op0=OP.mult).then_inc(s_v.h, 1)
                        v.wait_ge(s_s.h, h + 1)
                        v.tensor_copy(dwbc[:, h * C:(h + 1) * C], tmpm[:, :])

                @blk.scalar
                def _(sc):
                    for h in range(8):
                        sc.wait_ge(s_v.h, h + 1)
                        sc.activation(out=tmpm[:, :], in_=tmpm[:, :], func=AF.Exp).then_inc(s_s.h, 1)

        wwT = sb(nc.sbuf_tensor("d_wwT", [128, NT * D], BF16))
        wbT = sb(nc.sbuf_tensor("d_wbT", [128, NT * 16], BF16))
        with contextlib.ExitStack() as c1:
            wn = c1.enter_context(nc.sbuf_tensor("d_wn", [128, NT * D], BF16))
            wb = c1.enter_context(nc.sbuf_tensor("d_wb", [16, D], BF16))
            s_wl = Sem(c1, nc, "d_wl")
            s_sc = Sem(c1, nc, "d_sc")
            s_tr = Sem(c1, nc, "d_tr")
            s_mz = Sem(c1, nc, "d_mz")
            with nc.Block() as blk:
                @blk.vector
                def _(v):
                    v.memset(wb[:, :], 0.0).then_inc(s_mz.h, 1)

                @blk.gpsimd
                def _(g):
                    g.dma_start(out=wn.rearrange("p (t n) -> p t n", t=NT),
                                in_=wts["W_write"].rearrange("(t p) n -> p t n", p=128)).then_inc(s_wl.h, 16)
                    g.wait_ge(s_mz.h, 1)
                    g.dma_start(out=wb[0:8, :], in_=wts["W_beta"][:, :]).then_inc(s_wl.h, 16)

                @blk.sync
                def _(s):
                    s.wait_ge(s_wl.h, 16)
                    s.dma_start(out=scr_hbm.rearrange("(t p) n -> p t n", p=128),
                                in_=wn.rearrange("p (t n) -> p t n", t=NT)).then_inc(s_sc.h, 16)
                    s.wait_ge(s_wl.h, 32)
                    s.wait_ge(s_sc.h, 16)
                    for i in range(NT):
                        s.dma_start_transpose(wwT[:, i * D:(i + 1) * D],
                                              scr_hbm[:, i * 128:(i + 1) * 128]).then_inc(s_tr.h, 16)
                    s.dma_start(out=scr_hbm[0:16, :], in_=wb[:, :]).then_inc(s_sc.h, 16)
                    s.wait_ge(s_sc.h, 32)
                    for i in range(NT):
                        s.dma_start_transpose(wbT[:, i * 16:(i + 1) * 16],
                                              scr_hbm[0:16, i * 128:(i + 1) * 128]).then_inc(s_tr.h, 16)
                    s.wait_ge(s_tr.h, 16 * 2 * NT)

        with contextlib.ExitStack() as c2:
            xc = c2.enter_context(nc.sbuf_tensor("d_xc", [128, 2 * NT * CH], BF16))
            ksc = c2.enter_context(nc.sbuf_tensor("d_ksc", [128, 2 * NT * CH], BF16))
            vbsc = c2.enter_context(nc.sbuf_tensor("d_vb", [128, 2 * 4 * D], BF16))
            bsc = c2.enter_context(nc.sbuf_tensor("d_bsc", [128, 2 * 4 * 8], F32))
            sqb = c2.enter_context(nc.sbuf_tensor("d_sq", [128, CH], BF16))
            nrm = c2.enter_context(nc.sbuf_tensor("d_nrm", [1, 8 * CH], F32))
            rinvb = c2.enter_context(nc.sbuf_tensor("d_ri", [1, 8 * CH], BF16))
            Cb = c2.enter_context(nc.sbuf_tensor("d_Cb", [128, 2 * 512], BF16))
            Db = c2.enter_context(nc.sbuf_tensor("d_Db", [128, 2 * 512], BF16))
            Pb = c2.enter_context(nc.sbuf_tensor("d_Pb", [128, 2 * 512], BF16))
            A0b = c2.enter_context(nc.sbuf_tensor("d_A0", [128, 512], BF16))
            ATb = c2.enter_context(nc.sbuf_tensor("d_AT", [128, 512], BF16))
            A0Tb = c2.enter_context(nc.sbuf_tensor("d_A0T", [128, 512], BF16))
            atnb = c2.enter_context(nc.sbuf_tensor("d_atn", [128, 512], BF16))
            mskb = c2.enter_context(nc.sbuf_tensor("d_mb", [128, 512], BF16))
            ktm = c2.enter_context(nc.sbuf_tensor("d_ktm", [128, 1024], BF16))
            kbde = c2.enter_context(nc.sbuf_tensor("d_kbde", [128, 1024], BF16))
            kdw = c2.enter_context(nc.sbuf_tensor("d_kdw", [128, 1024], BF16))
            kde = c2.enter_context(nc.sbuf_tensor("d_kde", [128, 1024], BF16))
            kcdT = c2.enter_context(nc.sbuf_tensor("d_kcdT", [128, 1024], BF16))
            usb = c2.enter_context(nc.sbuf_tensor("d_usb", [128, 1024], BF16))
            vnew = c2.enter_context(nc.sbuf_tensor("d_vn", [128, 1024], BF16))
            bde = c2.enter_context(nc.sbuf_tensor("d_bde", [128, 16], F32))
            Ssb = c2.enter_context(nc.sbuf_tensor("d_S", [128, 8 * 128], BF16))
            PA = c2.enter_context(nc.psum_tensor("d_PA", [128, 1024], BF16))
            PA2 = c2.enter_context(nc.psum_tensor("d_PA2", [128, 512], BF16))
            PPX = c2.enter_context(nc.psum_tensor("d_PPX", [128, 512], F32))
            PC = c2.enter_context(nc.psum_tensor("d_PC", [128, 512], F32))
            PD = c2.enter_context(nc.psum_tensor("d_PD", [128, 512], F32))
            PU = c2.enter_context(nc.psum_tensor("d_PU", [128, 1024], F32))
            PS = c2.enter_context(nc.psum_tensor("d_PS", [128, 512], F32))

            if bz is not None:
                bzw1h, bzscr1h, bzw2h, bzscr2h = bz
                bzw = c2.enter_context(nc.sbuf_tensor("bz_w", [128, 8192], BF16))
                bzas = c2.enter_context(nc.sbuf_tensor("bz_as", [128, 40], F32))
                s_bwl = Sem(ctx, nc, "bz_wl")
                s_bred = Sem(ctx, nc, "bz_red")
                s_bsgn = Sem(ctx, nc, "bz_sgn")
                s_bts = Sem(ctx, nc, "bz_ts")
                s_bscr = Sem(ctx, nc, "bz_scr")
                BZG = [(bzw1h, bzscr1h, gi * 8, 8, D, gi * 8) for gi in range(4)]
                BZG += [(bzw2h, bzscr2h, gi * 2, 2, FF, 32 + gi * 2) for gi in range(4)]
                BZCUM = [8, 16, 24, 32, 34, 36, 38, 40]

                def bz_tile(gi, i):
                    cols = BZG[gi][4]
                    return bzw[:, i * cols:(i + 1) * cols]

                def bz_load(g, gi):
                    w, _, t0, ntl, cols, _ = BZG[gi]
                    if gi >= 1:
                        g.wait_ge(s_bscr.h, 16 * gi)
                    g.dma_start(
                        out=bzw[:, 0:ntl * cols].rearrange("p (t n) -> p t n", t=ntl),
                        in_=w.rearrange("(t p) n -> p t n", p=128)[:, t0:t0 + ntl, :],
                    ).then_inc(s_bwl.h, 16)

                def bz_store(s, gi):
                    _, scr, t0, ntl, cols, _ = BZG[gi]
                    s.wait_ge(s_bts.h, BZCUM[gi])
                    s.dma_start(
                        out=scr.rearrange("(t p) n -> p t n", p=128)[:, t0:t0 + ntl, :],
                        in_=bzw[:, 0:ntl * cols].rearrange("p (t n) -> p t n", t=ntl),
                    ).then_inc(s_bscr.h, 16)

                def bz_vec(v, gi):
                    _, _, t0, ntl, cols, a0 = BZG[gi]
                    for i in range(ntl):
                        if i == 0:
                            v.wait_ge(s_bwl.h, 16 * (gi + 1))
                        v.tensor_reduce(out=bzas[:, a0 + i:a0 + i + 1], in_=bz_tile(gi, i),
                                        axis=AX.X, op=OP.add,
                                        apply_absolute_value=True).then_inc(s_bred.h, 1)
                    for i in range(ntl):
                        v.wait_ge(s_bsgn.h, BZCUM[gi] - ntl + i + 1)
                        v.tensor_scalar(out=bz_tile(gi, i), in0=bz_tile(gi, i),
                                        scalar1=bzas[:, a0 + i:a0 + i + 1], scalar2=1.0 / cols,
                                        op0=OP.mult, op1=OP.mult).then_inc(s_bts.h, 1)

                def bz_scal(sc, gi):
                    _, _, t0, ntl, cols, _ = BZG[gi]
                    for i in range(ntl):
                        sc.wait_ge(s_bred.h, BZCUM[gi] - ntl + i + 1)
                        sc.activation(out=bz_tile(gi, i), in_=bz_tile(gi, i),
                                      func=AF.Sign).then_inc(s_bsgn.h, 1)

            s_x = Sem(c2, nc, "d_x")
            s_bm = Sem(c2, nc, "d_bm")
            s_bs = Sem(c2, nc, "d_bs")
            s_vm = Sem(c2, nc, "d_vm")
            s_ve = Sem(c2, nc, "d_ve")
            s_nm = Sem(c2, nc, "d_nm")
            s_nc = Sem(c2, nc, "d_nc")
            s_kb = Sem(c2, nc, "d_kb")
            s_kt = Sem(c2, nc, "d_kt")
            s_sc2 = Sem(c2, nc, "d_sc2")
            s_tp = Sem(c2, nc, "d_tp")
            s_kk = Sem(c2, nc, "d_kk")
            s_ms = Sem(c2, nc, "d_ms")
            s_a0 = Sem(c2, nc, "d_a0")
            s_a0t = Sem(c2, nc, "d_a0t")
            s_a0tc = Sem(c2, nc, "d_a0tc")
            s_lv = Sem(c2, nc, "d_lv")
            s_lvv = Sem(c2, nc, "d_lvv")
            s_ktc = Sem(c2, nc, "d_ktc")
            s_kc = Sem(c2, nc, "d_kc")
            s_kcc = Sem(c2, nc, "d_kcc")
            s_um = Sem(c2, nc, "d_um")
            s_uc = Sem(c2, nc, "d_uc")
            s_ks = Sem(c2, nc, "d_ks")
            s_vn = Sem(c2, nc, "d_vnn")
            s_om = Sem(c2, nc, "d_om")
            s_oe = Sem(c2, nc, "d_oe")
            s_dm = Sem(c2, nc, "d_dm")
            s_mg = Sem(c2, nc, "d_mg")
            s_si = Sem(c2, nc, "d_si")
            c_ktm = Sem(c2, nc, "d_cktm")
            c_a0t = Sem(c2, nc, "d_ca0t")
            c_cd = Sem(c2, nc, "d_ccd")
            c_kcc = Sem(c2, nc, "d_ckcc")
            c_uc = Sem(c2, nc, "d_cuc")
            c_oe = Sem(c2, nc, "d_coe")

            def sc_of(i):
                return i // 8, i % 8

            with nc.Block() as blk:
                if precast or bz is not None:
                    @blk.gpsimd
                    def _(g):
                        if bz is not None:
                            for gi in range(8):
                                bz_load(g, gi)
                        if precast:
                            for dst, srcw in precast:
                                g.dma_start(out=dst, in_=srcw).then_inc(s_pc.h, 16)

                @blk.sync
                def _(s):
                    for sc_ in range(NSC):
                        sp = sc_ % 2
                        if sc_ >= 2:
                            s.wait_ge(s_sc2.h, sc_ - 1)
                        s.dma_start(
                            out=xc[:, sp * NT * CH:(sp + 1) * NT * CH].rearrange("p (t n) -> p t n", t=NT),
                            in_=fmv(xbf_hbm, sc_ * CH, (sc_ + 1) * CH)).then_inc(s_x.h, 16)
                        if bz is not None and sc_ >= 1:
                            bz_store(s, sc_ - 1)
                    if bz is not None:
                        bz_store(s, NSC - 1)

                @blk.tensor
                def _(t):
                    t.wait_ge(s_cst.h, 1)
                    for sc_ in range(NSC):
                        sp = sc_ % 2
                        b = sc_ // 4
                        t.wait_ge(s_x.h, 16 * (sc_ + 1))
                        if sc_ >= 2:
                            t.wait_ge(s_sc2.h, sc_ - 1)
                        if sc_ > 0:
                            t.wait_ge(s_a0.h, 4 * sc_)
                            t.wait_ge(s_oe.h, 8 * sc_)
                        for ti in range(4):
                            for k in range(NT):
                                mm = t.matmul(PPX[:, ti * 8:(ti + 1) * 8],
                                              xc[:, (sp * NT + k) * CH + ti * 128:(sp * NT + k) * CH + (ti + 1) * 128],
                                              wbT[:, k * 16:k * 16 + 8],
                                              start=(k == 0), stop=(k == NT - 1))
                        mm.then_inc(s_bm.h, 1)
                        t.wait_ge(s_vn.h, 16 * sc_)
                        for g8 in range(8):
                            ti, hf = g8 // 2, g8 % 2
                            if sc_ * 8 + g8 >= 2:
                                t.wait_ge(s_ve.h, sc_ * 8 + g8 - 1)
                            for k in range(NT):
                                mm = t.matmul(PU[:, (g8 % 2) * CH:(g8 % 2 + 1) * CH],
                                              xc[:, (sp * NT + k) * CH + ti * 128:(sp * NT + k) * CH + (ti + 1) * 128],
                                              wwT[:, k * D + hf * CH:k * D + (hf + 1) * CH],
                                              start=(k == 0), stop=(k == NT - 1))
                            mm.then_inc(s_vm.h, 1)
                        for h in range(8):
                            t.wait_ge(s_bs.h, sc_ * 10 + 1 + h + 1)
                            if sc_ * 8 + h >= 1:
                                t.wait_ge(s_nc.h, sc_ * 8 + h)
                            t.matmul(PPX[0:1, :], cst["onesc"][:, :], sqb[:, :],
                                     start=True, stop=True).then_inc(s_nm.h, 1)
                        for h in range(8):
                            t.wait_ge(s_bs.h, sc_ * 10 + 10)
                            if sc_ * 8 + h >= 1:
                                t.wait_ge(s_kt.h, sc_ * 8 + h)
                            t.matmul(PPX[:, :], cst["onesr"][:, :], rinvb[:, h * CH:(h + 1) * CH],
                                     start=True, stop=True).then_inc(s_kb.h, 1)
                        for hp in range(4):
                            pr = sc_ * 4 + hp
                            t.wait_ge(s_kt.h, 8 * (sc_ + 1))
                            t.wait_ge(s_ktc.h, pr)
                            t.wait_ge(s_a0tc.h, pr)
                            for nl2 in range(2):
                                nl = hp * 2 + nl2
                                cb = 64 * nl2
                                for h in range(8):
                                    tp = t.transpose(PA[cb:cb + 64, h * 128:(h + 1) * 128],
                                                     ksc[:, (sp * NT + h) * CH + nl * 64:(sp * NT + h) * CH + (nl + 1) * 64],
                                                     cst["identb"][:, :])
                            tp.then_inc(s_tp.h, 1)
                            t.wait_ge(s_a0.h, 4 * sc_ + hp)
                            if hp > 0:
                                t.wait_ge(s_oe.h, 8 * sc_ + 2 * hp)
                            for nl2 in range(2):
                                nl = hp * 2 + nl2
                                cb = 64 * nl2
                                for h in range(8):
                                    mm = t.matmul(PPX[cb:cb + 64, h * 64:(h + 1) * 64],
                                                  ksc[:, (sp * NT + h) * CH + nl * 64:(sp * NT + h) * CH + (nl + 1) * 64],
                                                  ksc[:, (sp * NT + h) * CH + nl * 64:(sp * NT + h) * CH + (nl + 1) * 64],
                                                  start=True, stop=True)
                            mm.then_inc(s_kk.h, 1)
                            t.wait_ge(s_a0.h, 4 * sc_ + hp + 1)
                            for nl2 in range(2):
                                cb = 64 * nl2
                                for h in range(8):
                                    tp = t.transpose(PA2[cb:cb + 64, h * 64:(h + 1) * 64],
                                                     A0b[cb:cb + 64, h * 64:(h + 1) * 64],
                                                     cst["eye2"][cb:cb + 64, :])
                            tp.then_inc(s_a0t.h, 1)
                            for lv in range(5):
                                t.wait_ge(c_cd.h, 5 * pr + lv)
                                if lv == 0:
                                    t.wait_ge(c_a0t.h, pr + 1)
                                for nl2 in range(2):
                                    cb = 64 * nl2
                                    lc = ((lv - 1) % 2) * 512
                                    for h in range(8):
                                        lhs = (A0Tb[cb:cb + 64, h * 64:(h + 1) * 64] if lv == 0
                                               else Db[cb:cb + 64, lc + h * 64:lc + (h + 1) * 64])
                                        rhs = (A0b[cb:cb + 64, h * 64:(h + 1) * 64] if lv == 0
                                               else Cb[cb:cb + 64, lc + h * 64:lc + (h + 1) * 64])
                                        mm = t.matmul(PC[cb:cb + 64, h * 64:(h + 1) * 64], lhs, rhs,
                                                      start=True, stop=True)
                                mm.then_inc(s_lv.h, 1 if lv < 4 else 2)
                                if lv < 4:
                                    for nl2 in range(2):
                                        cb = 64 * nl2
                                        lc = ((lv - 1) % 2) * 512
                                        for h in range(8):
                                            lhs = (A0b[cb:cb + 64, h * 64:(h + 1) * 64] if lv == 0
                                                   else Cb[cb:cb + 64, lc + h * 64:lc + (h + 1) * 64])
                                            rhs = (A0Tb[cb:cb + 64, h * 64:(h + 1) * 64] if lv == 0
                                                   else Db[cb:cb + 64, lc + h * 64:lc + (h + 1) * 64])
                                            mm = t.matmul(PD[cb:cb + 64, h * 64:(h + 1) * 64], lhs, rhs,
                                                          start=True, stop=True)
                                    mm.then_inc(s_lv.h, 1)
                                t.wait_ge(c_cd.h, 5 * pr + lv + 1)
                                t.wait_ge(s_lvv.h, 5 * pr + lv)
                                if lv == 0:
                                    t.wait_ge(s_a0tc.h, pr + 1)
                                lp = (lv % 2) * 512
                                lpp = ((lv - 1) % 2) * 512
                                for nl2 in range(2):
                                    cb = 64 * nl2
                                    for h in range(8):
                                        mm = t.matmul(PPX[cb:cb + 64, h * 64:(h + 1) * 64],
                                                      Cb[cb:cb + 64, lp + h * 64:lp + (h + 1) * 64],
                                                      (Pb[cb:cb + 64, lpp + h * 64:lpp + (h + 1) * 64] if lv > 0
                                                       else Pb[cb:cb + 64, 512 + h * 64:512 + (h + 1) * 64]),
                                                      start=True, stop=True)
                                mm.then_inc(s_lv.h, 1)
                            t.wait_ge(s_ktc.h, pr + 1)
                            t.wait_ge(s_lvv.h, 5 * pr + 5)
                            t.wait_ge(c_kcc.h, pr)
                            for nl2 in range(2):
                                cb = 64 * nl2
                                ps = PC if nl2 == 0 else PD
                                for h in range(8):
                                    mm = t.matmul(ps[:, h * 64:(h + 1) * 64],
                                                  kbde[cb:cb + 64, h * 128:(h + 1) * 128],
                                                  ATb[cb:cb + 64, h * 64:(h + 1) * 64],
                                                  start=True, stop=True)
                            mm.then_inc(s_kc.h, 1)
                            t.wait_ge(s_ve.h, 8 * (sc_ + 1))
                            t.wait_ge(c_uc.h, pr)
                            if pr > 0:
                                t.wait_ge(s_ks.h, 2 * pr)
                            for nl2 in range(2):
                                nl = hp * 2 + nl2
                                cb = 64 * nl2
                                til = nl // 2
                                for h in range(8):
                                    mm = t.matmul(PU[cb:cb + 64, h * 128:(h + 1) * 128],
                                                  ATb[cb:cb + 64, h * 64:(h + 1) * 64],
                                                  vbsc[cb:cb + 64, sp * 4 * D + til * D + h * 128:sp * 4 * D + til * D + (h + 1) * 128],
                                                  start=True, stop=True)
                            mm.then_inc(s_um.h, 1)
                            for nl2 in range(2):
                                nl = hp * 2 + nl2
                                i = sc_ * 8 + nl
                                cb = 64 * nl2
                                t.wait_ge(c_uc.h, pr + 1)
                                t.wait_ge(c_kcc.h, pr + 1)
                                t.wait_ge(s_mg.h, 2 * i)
                                t.wait_ge(s_si.h, b + 1)
                                for h in range(8):
                                    mm = t.matmul(PU[cb:cb + 64, h * 128:(h + 1) * 128],
                                                  kcdT[:, nl2 * 512 + h * 64:nl2 * 512 + (h + 1) * 64],
                                                  Ssb[:, h * 128:(h + 1) * 128],
                                                  start=True, stop=True)
                                mm.then_inc(s_ks.h, 1)
                                t.wait_ge(s_vn.h, 4 * pr + 3 + nl2)
                                if nl2 == 1:
                                    t.wait_ge(s_oe.h, i)
                                for h in range(8):
                                    t.matmul(PPX[:, h * 64:(h + 1) * 64],
                                             Ssb[:, h * 128:(h + 1) * 128],
                                             kde[:, nl2 * 512 + h * 64:nl2 * 512 + (h + 1) * 64],
                                             start=True, stop=False)
                                    mm = t.matmul(PPX[:, h * 64:(h + 1) * 64],
                                                  vnew[cb:cb + 64, h * 128:(h + 1) * 128],
                                                  atnb[cb:cb + 64, h * 64:(h + 1) * 64],
                                                  start=False, stop=True)
                                mm.then_inc(s_om.h, 1)
                                for r in range(2):
                                    t.wait_ge(s_mg.h, 2 * i + r)
                                    for h4 in range(4):
                                        h = r * 4 + h4
                                        mm = t.matmul(PS[:, h4 * 128:(h4 + 1) * 128],
                                                      kdw[cb:cb + 64, h * 128:(h + 1) * 128],
                                                      vnew[cb:cb + 64, h * 128:(h + 1) * 128],
                                                      start=True, stop=True)
                                    mm.then_inc(s_dm.h, 1)
                        t.nop().then_inc(s_sc2.h, 1)

                @blk.scalar
                def _(sc):
                    for sc_ in range(NSC):
                        sp = sc_ % 2
                        if bz is not None:
                            bz_scal(sc, sc_)
                        sc.wait_ge(s_bm.h, sc_ + 1)
                        if sc_ >= 2:
                            sc.wait_ge(s_a0.h, 4 * (sc_ - 1))
                        sc.activation(out=bsc[:, sp * 32:(sp + 1) * 32], in_=PPX[:, 0:32],
                                      func=AF.Sigmoid).then_inc(s_bs.h, 1)
                        for h in range(8):
                            if sc_ * 8 + h >= 1:
                                sc.wait_ge(s_nm.h, sc_ * 8 + h)
                            sc.activation(out=sqb[:, :],
                                          in_=xc[:, (sp * NT + h) * CH:(sp * NT + h + 1) * CH],
                                          func=AF.Square).then_inc(s_bs.h, 1)
                        sc.wait_ge(s_nc.h, 8 * (sc_ + 1))
                        if sc_ > 0:
                            sc.wait_ge(s_kb.h, 8 * sc_)
                        sc.activation(out=nrm[:, :], in_=nrm[:, :], func=AF.Ln)
                        sc.drain()
                        sc.activation(out=rinvb[:, :], in_=nrm[:, :], func=AF.Exp,
                                      scale=-0.5).then_inc(s_bs.h, 1)
                        for hp in range(4):
                            pr = sc_ * 4 + hp
                            sc.wait_ge(s_tp.h, pr + 1)
                            if pr > 0:
                                sc.wait_ge(s_ktc.h, pr)
                            sc.copy(ktm[:, :], PA[:, 0:1024]).then_inc(c_ktm.h, 1)
                            sc.wait_ge(s_a0t.h, pr + 1)
                            sc.copy(A0Tb[:, :], PA2[:, :]).then_inc(c_a0t.h, 1)
                            for lv in range(5):
                                lp = (lv % 2) * 512
                                sc.wait_ge(s_lv.h, 15 * pr + 3 * lv + 2)
                                sc.copy(Cb[:, lp:lp + 512], PC[:, :])
                                if lv < 4:
                                    sc.copy(Db[:, lp:lp + 512], PD[:, :]).then_inc(c_cd.h, 1)
                                else:
                                    sc.copy(Cb[0:1, lp:lp + 1], PC[0:1, 0:1]).then_inc(c_cd.h, 1)
                            sc.wait_ge(s_kc.h, pr + 1)
                            sc.copy(kcdT[:, 0:512], PC[:, :])
                            sc.copy(kcdT[:, 512:1024], PD[:, :]).then_inc(c_kcc.h, 1)
                            sc.wait_ge(s_um.h, pr + 1)
                            if pr > 0:
                                sc.wait_ge(s_vn.h, 4 * pr)
                            sc.copy(usb[:, :], PU[:, :]).then_inc(c_uc.h, 1)
                            for nl2 in range(2):
                                nl = hp * 2 + nl2
                                i = sc_ * 8 + nl
                                sc.wait_ge(s_om.h, i + 1)
                                colT = sc_ * CH + nl * 64
                                for h in range(8):
                                    cp = sc.copy(ofm[:, h * TC + colT:h * TC + colT + 64],
                                                 PPX[:, h * 64:(h + 1) * 64])
                                cp.then_inc(s_oe.h, 1)

                @blk.vector
                def _(v):
                    for sc_ in range(NSC):
                        sp = sc_ % 2
                        b = sc_ // 4
                        if bz is not None:
                            bz_vec(v, sc_)
                        if sc_ % 4 == 0:
                            if sc_ > 0:
                                v.wait_ge(s_om.h, 8 * sc_)
                            v.memset(Ssb[:, :], 0.0)
                            v.drain()
                            v.tensor_copy(Ssb[0:1, 0:1], Ssb[0:1, 0:1]).then_inc(s_si.h, 1)
                        v.wait_ge(s_bs.h, sc_ * 10 + 1)
                        for g8 in range(8):
                            ti, hf = g8 // 2, g8 % 2
                            v.wait_ge(s_vm.h, sc_ * 8 + g8 + 1)
                            for q in range(4):
                                hh = hf * 4 + q
                                ts_ = v.tensor_scalar(
                                    out=vbsc[:, sp * 4 * D + ti * D + hh * 128:sp * 4 * D + ti * D + (hh + 1) * 128],
                                    in0=PU[:, (g8 % 2) * CH + q * 128:(g8 % 2) * CH + (q + 1) * 128],
                                    scalar1=bsc[:, sp * 32 + ti * 8 + hh:sp * 32 + ti * 8 + hh + 1],
                                    scalar2=None, op0=OP.mult)
                            ts_.then_inc(s_ve.h, 1)
                        for h in range(8):
                            v.wait_ge(s_nm.h, sc_ * 8 + h + 1)
                            v.tensor_copy(nrm[:, h * CH:(h + 1) * CH], PPX[0:1, :]).then_inc(s_nc.h, 1)
                        for h in range(8):
                            v.wait_ge(s_kb.h, sc_ * 8 + h + 1)
                            v.tensor_tensor(ksc[:, (sp * NT + h) * CH:(sp * NT + h + 1) * CH],
                                            xc[:, (sp * NT + h) * CH:(sp * NT + h + 1) * CH],
                                            PPX[:, :], OP.mult).then_inc(s_kt.h, 1)
                        for hp in range(4):
                            pr = sc_ * 4 + hp
                            v.wait_ge(s_kk.h, pr + 1)
                            for nl2 in range(2):
                                nl = hp * 2 + nl2
                                cb = 64 * nl2
                                til = nl // 2
                                for h in range(8):
                                    v.tensor_scalar(out=mskb[cb:cb + 64, h * 64:(h + 1) * 64],
                                                    in0=masks[cb:cb + 64, h * C:(h + 1) * C],
                                                    scalar1=bsc[cb:cb + 64, sp * 32 + til * 8 + h:sp * 32 + til * 8 + h + 1],
                                                    scalar2=-1.0, op0=OP.mult, op1=OP.mult)
                            v.tensor_tensor(atnb[:, :], PPX[:, :], masks[:, 8 * C:16 * C], OP.mult)
                            v.tensor_tensor(A0b[:, :], PPX[:, :], mskb[:, :],
                                            OP.mult).then_inc(s_a0.h, 1)
                            for nl2 in range(2):
                                nl = hp * 2 + nl2
                                cb = 64 * nl2
                                til = nl // 2
                                v.tensor_tensor(bde[cb:cb + 64, 0:8],
                                                bsc[cb:cb + 64, sp * 32 + til * 8:sp * 32 + (til + 1) * 8],
                                                hcols[cb:cb + 64, 0:8], OP.mult)
                            v.wait_ge(c_ktm.h, pr + 1)
                            for h in range(8):
                                v.tensor_scalar(out=kdw[:, h * 128:(h + 1) * 128],
                                                in0=ktm[:, h * 128:(h + 1) * 128],
                                                scalar1=hcols[:, 8 + h:9 + h],
                                                scalar2=None, op0=OP.mult)
                            for h in range(8):
                                ktc = v.tensor_scalar(out=kbde[:, h * 128:(h + 1) * 128],
                                                      in0=ktm[:, h * 128:(h + 1) * 128],
                                                      scalar1=bde[:, h:h + 1],
                                                      scalar2=None, op0=OP.mult)
                            ktc.then_inc(s_ktc.h, 1)
                            v.wait_ge(c_a0t.h, pr + 1)
                            v.tensor_tensor(Pb[:, 512:1024], A0Tb[:, :],
                                            cst["eye2r"][:, :], OP.add).then_inc(s_a0tc.h, 1)
                            for lv in range(5):
                                lp = (lv % 2) * 512
                                v.wait_ge(s_lv.h, 15 * pr + 3 * lv + 3)
                                lpp = ((lv - 1) % 2) * 512
                                dst = ATb if lv == 4 else Pb
                                dof = 0 if lv == 4 else lp
                                v.tensor_tensor(dst[:, dof:dof + 512],
                                                Pb[:, lpp:lpp + 512],
                                                PPX[:, :], OP.add).then_inc(s_lvv.h, 1)
                            for nl2 in range(2):
                                nl = hp * 2 + nl2
                                for h in range(8):
                                    kdl = v.tensor_tensor(kde[:, nl2 * 512 + h * 64:nl2 * 512 + (h + 1) * 64],
                                                          ksc[:, (sp * NT + h) * CH + nl * 64:(sp * NT + h) * CH + (nl + 1) * 64],
                                                          masks[:, (16 + h) * C:(17 + h) * C], OP.mult)
                            kdl.then_inc(s_vn.h, 2)
                            for nl2 in range(2):
                                nl = hp * 2 + nl2
                                i = sc_ * 8 + nl
                                cb = 64 * nl2
                                v.wait_ge(c_uc.h, pr + 1)
                                v.wait_ge(s_ks.h, i + 1)
                                v.tensor_tensor(vnew[cb:cb + 64, :], usb[cb:cb + 64, :],
                                                PU[cb:cb + 64, :], OP.subtract).then_inc(s_vn.h, 1)
                                for r in range(2):
                                    v.wait_ge(s_dm.h, 2 * i + r + 1)
                                    for h4 in range(4):
                                        h = r * 4 + h4
                                        stt = v.scalar_tensor_tensor(
                                            out=Ssb[:, h * 128:(h + 1) * 128],
                                            in0=Ssb[:, h * 128:(h + 1) * 128],
                                            scalar=hcols[:, 16 + h:17 + h],
                                            in1=PS[:, h4 * 128:(h4 + 1) * 128],
                                            op0=OP.mult, op1=OP.add)
                                    stt.then_inc(s_mg.h, 1)
        if ln_out is None:
            with contextlib.ExitStack() as cC:
                wrT = cC.enter_context(nc.sbuf_tensor("d_wrT", [128, NT * D], BF16))
                wn2 = cC.enter_context(nc.sbuf_tensor("d_wn2", [128, NT * D], BF16))
                x0c = cC.enter_context(nc.sbuf_tensor("d_x0c", [128, 2 * NT * CH], F32))
                ysb = cC.enter_context(nc.sbuf_tensor("d_ysb", [128, NT * CH], F32))
                pj = cC.enter_context(nc.psum_tensor("d_pj", [128, 4 * CH], F32))
                s_wl = Sem(cC, nc, "e_wl")
                s_sc = Sem(cC, nc, "e_sc")
                s_tr = Sem(cC, nc, "e_tr")
                s_x = Sem(cC, nc, "e_x")
                s_mm = Sem(cC, nc, "e_mm")
                s_ev = Sem(cC, nc, "e_ev")
                s_y = Sem(cC, nc, "e_y")
                with nc.Block() as blk:
                    @blk.gpsimd
                    def _(g):
                        if bz is not None:
                            g.wait_ge(s_bscr.h, 16 * 8)
                        g.dma_start(out=wn2.rearrange("p (t n) -> p t n", t=NT),
                                    in_=wts["W_read"].rearrange("(t p) n -> p t n", p=128)).then_inc(s_wl.h, 16)
                        for c in range(NCH):
                            g.wait_ge(s_ev.h, NT * (c + 1))
                            g.dma_start(out=fmv(y_hbm, c * CH, (c + 1) * CH),
                                        in_=ysb.rearrange("p (t n) -> p t n", t=NT)).then_inc(s_y.h, 16)
                        g.wait_ge(s_y.h, 16 * NCH)

                    @blk.sync
                    def _(s):
                        s.wait_ge(s_sc.h, 16)
                        for i in range(NT):
                            s.dma_start_transpose(wrT[:, i * D:(i + 1) * D],
                                                  scr_hbm[:, i * 128:(i + 1) * 128]).then_inc(s_tr.h, 16)
                        for c in range(NCH):
                            par = c % 2
                            if c >= 2:
                                s.wait_ge(s_ev.h, NT * (c - 1))
                            s.dma_start(
                                out=x0c[:, par * NT * CH:(par + 1) * NT * CH].rearrange("p (t n) -> p t n", t=NT),
                                in_=fmv(xfm_hbm, c * CH, (c + 1) * CH)).then_inc(s_x.h, 16)

                    @blk.tensor
                    def _(t):
                        t.wait_ge(s_tr.h, 16 * NT)
                        for c in range(NCH):
                            for m in range(NT):
                                gi = c * NT + m
                                pq = gi % 4
                                if gi >= 4:
                                    t.wait_ge(s_ev.h, gi - 3)
                                for k in range(NT):
                                    mm = t.matmul(pj[:, pq * CH:(pq + 1) * CH],
                                                  wrT[:, k * D + m * 128:k * D + (m + 1) * 128],
                                                  ofm[:, k * TC + c * CH:k * TC + (c + 1) * CH],
                                                  start=(k == 0), stop=(k == NT - 1))
                                mm.then_inc(s_mm.h, 1)

                    @blk.vector
                    def _(v):
                        for c in range(NCH):
                            par = c % 2
                            v.wait_ge(s_x.h, 16 * (c + 1))
                            for m in range(NT):
                                gi = c * NT + m
                                pq = gi % 4
                                v.wait_ge(s_mm.h, gi + 1)
                                if c > 0 and m == 0:
                                    v.wait_ge(s_y.h, 16 * c)
                                v.scalar_tensor_tensor(
                                    out=ysb[:, m * CH:(m + 1) * CH],
                                    in0=x0c[:, (par * NT + m) * CH:(par * NT + m + 1) * CH],
                                    scalar=2.0, in1=pj[:, pq * CH:(pq + 1) * CH],
                                    op0=OP.mult, op1=OP.add).then_inc(s_ev.h, 1)
        else:
            x1_hbm, x1b_hbm = ln_out
            with contextlib.ExitStack() as cC:
                wrT = cC.enter_context(nc.sbuf_tensor("d_wrT", [128, NT * D], BF16))
                x0c = cC.enter_context(nc.sbuf_tensor("d_x0c", [128, NT * CH], F32))
                ysb = cC.enter_context(nc.sbuf_tensor("d_ysb", [128, 2 * NT * CH], F32))
                tbf = cC.enter_context(nc.sbuf_tensor("d_tbf", [128, NT * CH], BF16))
                sqb = cC.enter_context(nc.sbuf_tensor("d_sqb", [128, NT * CH], BF16))
                srow = cC.enter_context(nc.sbuf_tensor("d_srow", [1, 6 * CH], F32))
                epst = cC.enter_context(nc.sbuf_tensor("d_eps", [1, 1], F32))
                xob = cC.enter_context(nc.sbuf_tensor("d_xob", [128, NT * CH], F32))
                xbb = cC.enter_context(nc.sbuf_tensor("d_xbb", [128, NT * CH], BF16))
                pj = cC.enter_context(nc.psum_tensor("d_pj", [128, 4 * CH], F32))
                ps_st = cC.enter_context(nc.psum_tensor("d_pst", [64, CH], F32))
                ps_bc = cC.enter_context(nc.psum_tensor("d_pbc", [128, 2 * CH], F32))
                s_wl = Sem(cC, nc, "e_wl")
                s_sc = Sem(cC, nc, "e_sc")
                s_tr = Sem(cC, nc, "e_tr")
                s_x = Sem(cC, nc, "e_x")
                s_mm = Sem(cC, nc, "e_mm")
                s_ev = Sem(cC, nc, "e_ev")
                s_tbf = Sem(cC, nc, "e_tbf")
                s_sq = Sem(cC, nc, "e_sq")
                s_stats = Sem(cC, nc, "e_st")
                s_var = Sem(cC, nc, "e_var")
                s_rstd = Sem(cC, nc, "e_rs")
                s_bc = Sem(cC, nc, "e_bc")
                s_ap = Sem(cC, nc, "e_ap")
                s_out = Sem(cC, nc, "e_out")
                with nc.Block() as blk:
                    @blk.gpsimd
                    def _(g):
                        if bz is not None:
                            g.wait_ge(s_bscr.h, 16 * 8)
                        g.dma_start(out=scr_hbm[:, :], in_=wts["W_read"][:, :]).then_inc(s_sc.h, 16)
                        for c in range(NCH):
                            g.wait_ge(s_ap.h, c + 1)
                            g.dma_start(out=fmv(x1_hbm, c * CH, (c + 1) * CH),
                                        in_=xob.rearrange("p (t n) -> p t n", t=NT)).then_inc(s_out.h, 16)
                            g.dma_start(out=fmv(x1b_hbm, c * CH, (c + 1) * CH),
                                        in_=xbb.rearrange("p (t n) -> p t n", t=NT)).then_inc(s_out.h, 16)
                        g.wait_ge(s_out.h, 32 * NCH)

                    @blk.sync
                    def _(s):
                        s.wait_ge(s_sc.h, 16)
                        for i in range(NT):
                            s.dma_start_transpose(wrT[:, i * D:(i + 1) * D],
                                                  scr_hbm[:, i * 128:(i + 1) * 128]).then_inc(s_tr.h, 16)
                        for c in range(NCH):
                            if c >= 1:
                                s.wait_ge(s_ev.h, NT * c)
                            s.dma_start(
                                out=x0c.rearrange("p (t n) -> p t n", t=NT),
                                in_=fmv(xfm_hbm, c * CH, (c + 1) * CH)).then_inc(s_x.h, 16)

                    @blk.tensor
                    def _(t):
                        def ln_stats(cc):
                            t.wait_ge(s_tbf.h, cc + 1)
                            t.wait_ge(s_sq.h, cc + 1)
                            if cc > 0:
                                t.wait_ge(s_var.h, cc)
                            for i in range(NT):
                                t.matmul(ps_st[0:1, :], cst["onesc"][:, :], tbf[:, i * CH:(i + 1) * CH],
                                         start=(i == 0), stop=(i == NT - 1))
                            for i in range(NT):
                                mm = t.matmul(ps_st[32:33, :], cst["onesc"][:, :], sqb[:, i * CH:(i + 1) * CH],
                                              start=(i == 0), stop=(i == NT - 1))
                            mm.then_inc(s_stats.h, 1)

                        def ln_bcast(cc):
                            t.wait_ge(s_rstd.h, cc + 1)
                            if cc > 0:
                                t.wait_ge(s_ap.h, cc)
                            t.matmul(ps_bc[:, 0:CH], cst["onesrf"][:, :], srow[:, 0:CH],
                                     start=True, stop=True)
                            t.matmul(ps_bc[:, CH:2 * CH], cst["onesrf"][:, :], srow[:, 4 * CH:5 * CH],
                                     start=True, stop=True).then_inc(s_bc.h, 1)

                        t.wait_ge(s_tr.h, 16 * NT)
                        for c in range(NCH):
                            for m in range(NT):
                                gi = c * NT + m
                                pq = gi % 4
                                if gi >= 4:
                                    t.wait_ge(s_ev.h, gi - 3)
                                for k in range(NT):
                                    mm = t.matmul(pj[:, pq * CH:(pq + 1) * CH],
                                                  wrT[:, k * D + m * 128:k * D + (m + 1) * 128],
                                                  ofm[:, k * TC + c * CH:k * TC + (c + 1) * CH],
                                                  start=(k == 0), stop=(k == NT - 1))
                                mm.then_inc(s_mm.h, 1)
                            if c >= 1:
                                ln_stats(c - 1)
                                ln_bcast(c - 1)
                        ln_stats(NCH - 1)
                        ln_bcast(NCH - 1)

                    @blk.scalar
                    def _(scl):
                        for cc in range(NCH):
                            scl.wait_ge(s_ev.h, NT * (cc + 1))
                            if cc > 0:
                                scl.wait_ge(s_stats.h, cc)
                            ts_ = (cc % 2) * NT * CH
                            scl.activation(out=sqb[:, :], in_=ysb[:, ts_:ts_ + NT * CH],
                                           func=AF.Square).then_inc(s_sq.h, 1)
                            scl.wait_ge(s_var.h, cc + 1)
                            scl.activation(out=srow[:, 3 * CH:4 * CH], in_=srow[:, CH:2 * CH],
                                           func=AF.Ln, bias=epst[0:1, 0:1])
                            scl.activation(out=srow[:, 4 * CH:5 * CH], in_=srow[:, 3 * CH:4 * CH],
                                           func=AF.Exp, scale=-0.5).then_inc(s_rstd.h, 1)

                    @blk.vector
                    def _(v):
                        def ln_row(cc):
                            v.wait_ge(s_stats.h, cc + 1)
                            if cc > 0:
                                v.wait_ge(s_bc.h, cc)
                            v.tensor_scalar(out=srow[:, 0:CH], in0=ps_st[0:1, :], scalar1=1.0 / D,
                                            scalar2=None, op0=OP.mult)
                            v.drain()
                            v.tensor_tensor(srow[:, 2 * CH:3 * CH], srow[:, 0:CH], srow[:, 0:CH], OP.mult)
                            v.drain()
                            v.scalar_tensor_tensor(out=srow[:, CH:2 * CH], in0=ps_st[32:33, :],
                                                   scalar=1.0 / D, in1=srow[:, 2 * CH:3 * CH],
                                                   op0=OP.mult, op1=OP.subtract).then_inc(s_var.h, 1)

                        def ln_apply(cc):
                            v.wait_ge(s_bc.h, cc + 1)
                            if cc > 0:
                                v.wait_ge(s_out.h, 32 * cc)
                            ts_ = (cc % 2) * NT * CH
                            for i in range(NT):
                                v.tensor_tensor(xob[:, i * CH:(i + 1) * CH],
                                                ysb[:, ts_ + i * CH:ts_ + (i + 1) * CH],
                                                ps_bc[:, 0:CH], OP.subtract)
                            for i in range(NT):
                                v.tensor_tensor(xob[:, i * CH:(i + 1) * CH], xob[:, i * CH:(i + 1) * CH],
                                                ps_bc[:, CH:2 * CH], OP.mult)
                            v.tensor_copy(xbb[:, :], xob[:, :]).then_inc(s_ap.h, 1)

                        v.memset(epst[:, :], 1e-5)
                        for c in range(NCH):
                            par = c % 2
                            v.wait_ge(s_x.h, 16 * (c + 1))
                            ts_ = (c % 2) * NT * CH
                            for m in range(NT):
                                gi = c * NT + m
                                pq = gi % 4
                                v.wait_ge(s_mm.h, gi + 1)
                                if m == 0 and c >= 2:
                                    v.wait_ge(s_sq.h, c - 1)
                                v.scalar_tensor_tensor(
                                    out=ysb[:, ts_ + m * CH:ts_ + (m + 1) * CH],
                                    in0=x0c[:, m * CH:(m + 1) * CH],
                                    scalar=2.0, in1=pj[:, pq * CH:(pq + 1) * CH],
                                    op0=OP.mult, op1=OP.add).then_inc(s_ev.h, 1)
                            v.wait_ge(s_stats.h, c) if c >= 1 else None
                            v.tensor_copy(tbf[:, :], ysb[:, ts_:ts_ + NT * CH]).then_inc(s_tbf.h, 1)
                            if c >= 1:
                                ln_row(c - 1)
                                ln_apply(c - 1)
                        ln_row(NCH - 1)
                        ln_apply(NCH - 1)

CAP = 1536
NQ = CAP // CH


def emit_l1ffn_sparse(nc, cst, xbf_hbm, wts, y_hbm, scr, stages=7):
    x3tm = scr["x3tm"]
    lists2 = scr["lists2"]
    cwl = scr["cwl"]
    moeparts = scr["moeparts"]
    with contextlib.ExitStack() as ctx:
        sb = ctx.enter_context
        cwrp = sb(nc.sbuf_tensor("m_cwrp", [10, TC], BF16))
        d1bcols = sb(nc.sbuf_tensor("m_d1b", [128, 32], F32))
        d2bcols = sb(nc.sbuf_tensor("m_d2b", [128, 8], F32))
        emit_bias_cols(nc, cst, wts["d1b"], 32, d1bcols, 0, "md1b")
        emit_bias_cols(nc, cst, wts["d2b"], 8, d2bcols, 0, "md2b")

        with contextlib.ExitStack() as c1:
            wgr = c1.enter_context(nc.sbuf_tensor("m_wgr", [128, NT * 10], BF16))
            bgr = c1.enter_context(nc.sbuf_tensor("m_bgr", [1, 10], F32))
            bgrb = c1.enter_context(nc.sbuf_tensor("m_bgrb", [128, 10], F32))
            xg = c1.enter_context(nc.sbuf_tensor("m_xg", [128, 2 * NT * CH], BF16))
            gl = c1.enter_context(nc.sbuf_tensor("m_gl", [128, 32 * 10], F32))
            rowb = c1.enter_context(nc.sbuf_tensor("m_rb", [128, 8 * 32], F32))
            mtmp = c1.enter_context(nc.sbuf_tensor("m_mt", [128, 20], F32))
            mk1 = c1.enter_context(nc.sbuf_tensor("m_mk1", [128, 32 * 8], F32))
            mk2 = c1.enter_context(nc.sbuf_tensor("m_mk2", [128, 32 * 8], F32))
            flg = c1.enter_context(nc.sbuf_tensor("m_flg", [8, TC], F32))
            one8 = c1.enter_context(nc.sbuf_tensor("m_one8", [8, TC], F32))
            cum = c1.enter_context(nc.sbuf_tensor("m_cum", [8, TC], F32))
            postm = c1.enter_context(nc.sbuf_tensor("m_ptm", [128, 32 * 8], F32))
            sct = c1.enter_context(nc.sbuf_tensor("m_sct", [128, 4], F32))
            offc = c1.enter_context(nc.sbuf_tensor("m_offc", [128, 4], F32))
            offu = c1.enter_context(nc.sbuf_tensor("m_offu", [128, 1], U32))
            fill = c1.enter_context(nc.sbuf_tensor("m_fill", [128, 192], F32))
            zerob = c1.enter_context(nc.sbuf_tensor("m_zb", [128, 2 * D], BF16))
            xtm = c1.enter_context(nc.sbuf_tensor("m_xtm", [128, 2 * D], BF16))
            pgl = c1.enter_context(nc.psum_tensor("m_pgl", [128, 2 * CH], F32))
            pbc = c1.enter_context(nc.psum_tensor("m_pbc", [128, 16], F32))
            ptr = c1.enter_context(nc.psum_tensor("m_ptr", [16, 128], F32))
            ptm = c1.enter_context(nc.psum_tensor("m_ptm2", [128, 16], F32))

            s_wg = Sem(c1, nc, "m_wg")
            s_xg = Sem(c1, nc, "m_xg")
            s_gm = Sem(c1, nc, "m_gm")
            s_ge = Sem(c1, nc, "m_ge")
            s_top = Sem(c1, nc, "m_top")
            s_bgr = Sem(c1, nc, "m_bgr2")
            s_fin = Sem(c1, nc, "m_fin")
            s_scan = Sem(c1, nc, "m_scan")
            s_ptt = Sem(c1, nc, "m_ptt")
            s_ptc = Sem(c1, nc, "m_ptc")
            s_off = Sem(c1, nc, "m_off")
            s_sc2 = Sem(c1, nc, "m_sc3")
            s_xt = Sem(c1, nc, "m_xt")
            s_xts = Sem(c1, nc, "m_xts")
            rbnd = c1.enter_context(nc.gpsimd.register("m_rbnd"))

            with nc.Block() as blk:
                @blk.gpsimd
                def _(g):
                    g.reg_mov(rbnd, 8 * CAP - 1)
                    g.dma_start(out=wgr.rearrange("p (t n) -> p t n", t=NT)[:, :, 0:8],
                                in_=wts["Wg"].rearrange("(t p) e -> p t e", p=128)).then_inc(s_wg.h, 16)
                    g.dma_start(out=wgr.rearrange("p (t n) -> p t n", t=NT)[:, :, 8:10],
                                in_=wts["Wr"].rearrange("(t p) e -> p t e", p=128)).then_inc(s_wg.h, 16)
                    g.dma_start(out=bgr[:, 0:8], in_=wts["bg"].rearrange("(o e) -> o e", o=1)).then_inc(s_wg.h, 16)
                    g.dma_start(out=bgr[:, 8:10], in_=wts["br"].rearrange("(o e) -> o e", o=1)).then_inc(s_wg.h, 16)
                    g.wait_ge(s_scan.h, 1)
                    for r in range(32):
                        g.dma_start(out=moeparts.rearrange("(a p) d -> p a d", p=128)[:, r * 2:(r + 1) * 2, :],
                                    in_=zerob.rearrange("p (a d) -> p a d", a=2),
                                    ).then_inc(s_sc2.h, 16)
                    g.dma_start(out=lists2.rearrange("(a p) c -> p a c", p=128),
                                in_=fill[:, 0:192].rearrange("p (a c) -> p a c", a=96),
                                ).then_inc(s_sc2.h, 16)
                    for tt in range(32):
                        g.wait_ge(s_xt.h, 16 * (tt + 1))
                        g.dma_start(out=x3tm[tt * 128:(tt + 1) * 128, :],
                                    in_=xtm[:, (tt % 2) * D:(tt % 2 + 1) * D]).then_inc(s_xts.h, 16)
                    for ti in range(32):
                        for s_ in range(2):
                            k = ti * 2 + s_
                            g.wait_ge(s_off.h, 2 * (k + 1))
                            g.indirect_dma_start(
                                out=lists2[:, :],
                                out_offset=bass.IndirectOffsetOnAxis(ap=offu[:, :], axis=0),
                                in_=sct[:, 0:2], in_offset=None,
                                bounds_check=rbnd, oob_is_err=False).then_inc(s_sc2.h, 16)
                            g.indirect_dma_start(
                                out=cwl[:, :],
                                out_offset=bass.IndirectOffsetOnAxis(ap=offu[:, :], axis=0),
                                in_=sct[:, 2:3], in_offset=None,
                                bounds_check=rbnd, oob_is_err=False).then_inc(s_sc2.h, 16)
                            g.wait_ge(s_sc2.h, 16 * (33 + 2 * (k + 1)))
                    g.wait_ge(s_xts.h, 16 * 32)

                @blk.sync
                def _(s):
                    for q in range(NCH):
                        par = q % 2
                        if q >= 2:
                            s.wait_ge(s_gm.h, 1 + (q - 1) * 4)
                        s.dma_start(
                            out=xg[:, par * NT * CH:(par + 1) * NT * CH].rearrange("p (t n) -> p t n", t=NT),
                            in_=fmv(xbf_hbm, q * CH, (q + 1) * CH)).then_inc(s_xg.h, 16)
                    for tt in range(32):
                        if tt >= 2:
                            s.wait_ge(s_xts.h, 16 * (tt - 1))
                        s.dma_start_transpose(xtm[:, (tt % 2) * D:(tt % 2 + 1) * D],
                                              xbf_hbm[:, tt * 128:(tt + 1) * 128]).then_inc(s_xt.h, 16)

                @blk.tensor
                def _(t):
                    t.wait_ge(s_wg.h, 64)
                    t.matmul(pbc[:, 0:10], cst["onesrf"][:, :], bgr[:, :], start=True, stop=True).then_inc(s_gm.h, 1)
                    for ti in range(32):
                        q, j = ti // 4, ti % 4
                        par = q % 2
                        pp = ti % 2
                        t.wait_ge(s_xg.h, 16 * (q + 1))
                        if ti >= 2:
                            t.wait_ge(s_ge.h, ti - 1)
                        for k in range(NT):
                            mm = t.matmul(pgl[:, pp * CH:pp * CH + 10],
                                          xg[:, (par * NT + k) * CH + j * 128:(par * NT + k) * CH + (j + 1) * 128],
                                          wgr[:, k * 10:(k + 1) * 10],
                                          start=(k == 0), stop=(k == NT - 1))
                        mm.then_inc(s_gm.h, 1)
                    for ti in range(32):
                        t.wait_ge(s_fin.h, ti + 1)
                        if ti > 0:
                            t.wait_ge(s_fin.h, 32 + ti)
                        t.transpose(ptr[0:10, :], gl[:, ti * 10:(ti + 1) * 10],
                                    cst["ident"][:, :]).then_inc(s_gm.h, 1)
                    t.wait_ge(s_scan.h, 3)
                    for ti in range(32):
                        if ti > 0:
                            t.wait_ge(s_ptc.h, ti)
                        t.transpose(ptm[:, 0:8], cum[:, ti * 128:(ti + 1) * 128],
                                    cst["ident"][0:8, 0:8]).then_inc(s_ptt.h, 1)

                @blk.scalar
                def _(sc):
                    sc.wait_ge(s_gm.h, 1)
                    sc.copy(bgrb[:, :], pbc[:, 0:10]).then_inc(s_bgr.h, 1)
                    sc.wait_ge(s_top.h, 32)
                    sc.activation(out=rowb[:, 96:128], in_=rowb[:, 64:96], func=AF.Sigmoid)
                    sc.activation(out=rowb[:, 192:224], in_=rowb[:, 160:192],
                                  func=AF.Sigmoid).then_inc(s_bgr.h, 1)

                @blk.vector
                def _(v):
                    v.memset(fill[:, :], 65535.0)
                    v.memset(zerob[:, :], 0.0).then_inc(s_scan.h, 1)
                    v.memset(one8[:, :], 1.0)
                    for ti in range(32):
                        pp = ti % 2
                        v.wait_ge(s_gm.h, 1 + ti + 1)
                        if ti == 0:
                            v.wait_ge(s_bgr.h, 1)
                        v.tensor_tensor(gl[:, ti * 10:(ti + 1) * 10], pgl[:, pp * CH:pp * CH + 10],
                                        bgrb[:, :], OP.add).then_inc(s_ge.h, 1)
                        ge = gl[:, ti * 10:ti * 10 + 8]
                        v.drain()
                        v.tensor_reduce(out=rowb[:, ti:ti + 1], in_=ge, axis=AX.X, op=OP.max)
                        v.drain()
                        v.tensor_scalar(out=mtmp[:, 0:8], in0=ge, scalar1=rowb[:, ti:ti + 1],
                                        scalar2=None, op0=OP.is_ge)
                        v.drain()
                        v.scalar_tensor_tensor(out=mtmp[:, 10:18], in0=mtmp[:, 0:8], scalar=-1e30,
                                               in1=ge, op0=OP.mult, op1=OP.add)
                        v.drain()
                        v.tensor_reduce(out=rowb[:, 32 + ti:33 + ti], in_=mtmp[:, 10:18],
                                        axis=AX.X, op=OP.max)
                        v.drain()
                        v.tensor_tensor(rowb[:, 64 + ti:65 + ti], rowb[:, ti:ti + 1],
                                        rowb[:, 32 + ti:33 + ti], OP.subtract)
                        v.tensor_tensor(rowb[:, 160 + ti:161 + ti], gl[:, ti * 10 + 8:ti * 10 + 9],
                                        gl[:, ti * 10 + 9:ti * 10 + 10], OP.subtract).then_inc(s_top.h, 1)
                    v.wait_ge(s_bgr.h, 2)
                    v.tensor_scalar(out=rowb[:, 128:160], in0=rowb[:, 96:128], scalar1=-1.0,
                                    scalar2=1.0, op0=OP.mult, op1=OP.add)
                    v.tensor_scalar(out=rowb[:, 224:256], in0=rowb[:, 192:224], scalar1=-1.0,
                                    scalar2=1.0, op0=OP.mult, op1=OP.add)
                    for ti in range(32):
                        ge = gl[:, ti * 10:ti * 10 + 8]
                        v.tensor_scalar(out=mk1[:, ti * 8:(ti + 1) * 8], in0=ge,
                                        scalar1=rowb[:, ti:ti + 1], scalar2=None, op0=OP.is_ge)
                        v.drain()
                        v.scalar_tensor_tensor(out=mtmp[:, 10:18], in0=mk1[:, ti * 8:(ti + 1) * 8],
                                               scalar=-1e30, in1=ge, op0=OP.mult, op1=OP.add)
                        v.drain()
                        v.tensor_scalar(out=mk2[:, ti * 8:(ti + 1) * 8], in0=mtmp[:, 10:18],
                                        scalar1=rowb[:, 32 + ti:33 + ti], scalar2=None, op0=OP.is_ge)
                        v.drain()
                        v.tensor_scalar(out=ge, in0=mk1[:, ti * 8:(ti + 1) * 8],
                                        scalar1=rowb[:, 96 + ti:97 + ti], scalar2=None, op0=OP.mult)
                        v.drain()
                        v.scalar_tensor_tensor(out=ge, in0=mk2[:, ti * 8:(ti + 1) * 8],
                                               scalar=rowb[:, 128 + ti:129 + ti],
                                               in1=ge, op0=OP.mult, op1=OP.add)
                        v.drain()
                        v.tensor_copy(gl[:, ti * 10 + 8:ti * 10 + 9], rowb[:, 192 + ti:193 + ti])
                        v.tensor_copy(gl[:, ti * 10 + 9:ti * 10 + 10],
                                      rowb[:, 224 + ti:225 + ti]).then_inc(s_fin.h, 1)
                    for ti in range(32):
                        v.wait_ge(s_gm.h, 33 + ti + 1)
                        v.tensor_copy(cwrp[:, ti * 128:(ti + 1) * 128], ptr[0:10, :]).then_inc(s_fin.h, 1)
                    v.tensor_scalar(out=flg[:, :], in0=cwrp[0:8, :], scalar1=0.0,
                                    scalar2=None, op0=OP.is_gt)
                    v.drain()
                    v.tensor_tensor_scan(out=cum[:, :], data0=one8[:, :], data1=flg[:, :],
                                         initial=0.0, op0=OP.mult, op1=OP.add)
                    v.drain()
                    v.tensor_tensor(cum[:, :], cum[:, :], flg[:, :], OP.subtract).then_inc(s_scan.h, 2)
                    nsc = 0
                    for ti in range(32):
                        v.wait_ge(s_ptt.h, ti + 1)
                        v.tensor_copy(postm[:, ti * 8:(ti + 1) * 8], ptm[:, 0:8]).then_inc(s_ptc.h, 1)
                        v.drain()
                        v.tensor_tensor(postm[:, ti * 8:(ti + 1) * 8], postm[:, ti * 8:(ti + 1) * 8],
                                        cst["eoff8"][:, :], OP.add)
                        for s_ in range(2):
                            k = ti * 2 + s_
                            mk = mk1 if s_ == 0 else mk2
                            cwc = 96 if s_ == 0 else 128
                            if k >= 1:
                                v.wait_ge(s_sc2.h, 16 * (33 + 2 * k))
                            v.drain()
                            v.tensor_tensor(mtmp[:, 0:8],
                                            mk[:, ti * 8:(ti + 1) * 8],
                                            postm[:, ti * 8:(ti + 1) * 8], OP.mult)
                            v.drain()
                            v.tensor_reduce(out=offc[:, 0:1], in_=mtmp[:, 0:8], axis=AX.X, op=OP.add)
                            v.drain()
                            v.tensor_tensor(mtmp[:, 0:8], mk[:, ti * 8:(ti + 1) * 8],
                                            cst["eoff8"][:, :], OP.mult)
                            v.drain()
                            v.tensor_reduce(out=offc[:, 1:2], in_=mtmp[:, 0:8], axis=AX.X, op=OP.add)
                            v.drain()
                            v.tensor_tensor(offc[:, 2:3], offc[:, 0:1], offc[:, 1:2], OP.subtract)
                            v.drain()
                            v.tensor_scalar(out=offc[:, 3:4], in0=offc[:, 2:3], scalar1=float(CAP),
                                            scalar2=None, op0=OP.is_ge)
                            v.drain()
                            v.scalar_tensor_tensor(out=offc[:, 0:1], in0=offc[:, 3:4], scalar=65536.0,
                                                   in1=offc[:, 0:1], op0=OP.mult, op1=OP.add)
                            v.drain()
                            v.tensor_copy(offu[:, :], offc[:, 0:1])
                            v.tensor_copy(sct[:, 0:1], cst["tokid32"][:, ti:ti + 1])
                            v.tensor_scalar(out=sct[:, 1:2], in0=cst["tokid32"][:, ti:ti + 1],
                                            scalar1=float(s_ * TC), scalar2=None, op0=OP.add)
                            v.tensor_copy(sct[:, 2:3], rowb[:, cwc + ti:cwc + ti + 1])
                            v.drain()
                            v.tensor_copy(sct[0:1, 0:1], sct[0:1, 0:1]).then_inc(s_off.h, 2)

        if not (stages & 2):
            return
        with contextlib.ExitStack() as c3:
            b1cols = c3.enter_context(nc.sbuf_tensor("m_b1c", [128, NE * 32], F32))
            b2cols = c3.enter_context(nc.sbuf_tensor("m_b2c", [128, NE * 8], F32))
            for e in range(NE):
                emit_bias_cols(nc, cst, wts["b1"][e, :], 32, b1cols, e * 32, f"sb1_{e}")
                emit_bias_cols(nc, cst, wts["b2"][e, :], 8, b2cols, e * 8, f"sb2_{e}")
            accb = c3.enter_context(nc.sbuf_tensor("m_acc", [128, 2 * CH], F32))
            u1 = c3.enter_context(nc.sbuf_tensor("m_u1", [128, CH], F32))
            cwx = c3.enter_context(nc.sbuf_tensor("m_cwx", [128, CH], F32))
            w1sb = c3.enter_context(nc.sbuf_tensor("m_w1", [128, NT * FF], BF16))
            w2sb = c3.enter_context(nc.sbuf_tensor("m_w2", [128, 32 * 1024], BF16))
            hsb = c3.enter_context(nc.sbuf_tensor("m_h", [128, 32 * CH], BF16))
            xfm = c3.enter_context(nc.sbuf_tensor("m_xfm", [128, NT * CH], BF16))
            xgt = c3.enter_context(nc.sbuf_tensor("m_xgt", [128, 2 * D], BF16))
            idxg = c3.enter_context(nc.sbuf_tensor("m_idx", [128, 2], U32))
            drow = c3.enter_context(nc.sbuf_tensor("m_drw", [128, 4], U32))
            cwb = c3.enter_context(nc.sbuf_tensor("m_cwb", [128, CH], F32))
            cwrow = c3.enter_context(nc.sbuf_tensor("m_cwr", [1, CH], F32))
            tmp = c3.enter_context(nc.sbuf_tensor("m_tmp", [128, NT * CH], BF16))
            etm = c3.enter_context(nc.sbuf_tensor("m_etm", [128, 2 * D], BF16))
            PA = c3.enter_context(nc.psum_tensor("m_PA", [128, 1024], BF16))
            ph = c3.enter_context(nc.psum_tensor("m_ph", [128, 2 * CH], F32))
            po = c3.enter_context(nc.psum_tensor("m_po", [128, 2 * CH], F32))
            PT = c3.enter_context(nc.psum_tensor("m_PT", [128, CH], BF16))
            pcw = c3.enter_context(nc.psum_tensor("m_pcw", [128, CH], F32))

            s_w1 = Sem(c3, nc, "s_w1")
            s_w2 = Sem(c3, nc, "s_w2")
            s_ix = Sem(c3, nc, "s_ix")
            s_gx = Sem(c3, nc, "s_gx")
            s_cr = Sem(c3, nc, "s_cr")
            s_it = Sem(c3, nc, "s_it")
            c_xf = Sem(c3, nc, "c_xf")
            s_hm = Sem(c3, nc, "s_hm")
            s_he = Sem(c3, nc, "s_he")
            s_cb = Sem(c3, nc, "s_cb")
            s_cv = Sem(c3, nc, "s_cv")
            s_om = Sem(c3, nc, "s_om")
            s_fl = Sem(c3, nc, "s_fl")
            s_ot = Sem(c3, nc, "s_ot")
            c_ot = Sem(c3, nc, "c_ot")
            s_sc = Sem(c3, nc, "s_sc")
            rbg = c3.enter_context(nc.gpsimd.register("s_rbg"))
            rbs = c3.enter_context(nc.gpsimd.register("s_rbs"))

            def eq(e, q):
                return e * NQ + q

            with nc.Block() as blk:
                @blk.gpsimd
                def _(g):
                    g.reg_mov(rbg, TC - 1)
                    g.reg_mov(rbs, 2 * TC - 1)
                    for e in range(NE):
                        if e > 0:
                            g.wait_ge(s_hm.h, 32 * NQ * e)
                        for qq in range(4):
                            g.dma_start(
                                out=w1sb[:, qq * 2 * FF:(qq + 1) * 2 * FF].rearrange("p (t n) -> p t n", t=2),
                                in_=wts["W1"][e * D:(e + 1) * D, :].rearrange("(t p) n -> p t n", p=128)[:, qq * 2:(qq + 1) * 2, :],
                            ).then_inc(s_w1.h, 16)
                        if e > 0:
                            g.wait_ge(s_om.h, 4 * NQ * e)
                        for qq in range(4):
                            g.dma_start(
                                out=w2sb[:, qq * 8 * 1024:(qq + 1) * 8 * 1024].rearrange("p (t n) -> p t n", t=8),
                                in_=wts["W2"][e * FF:(e + 1) * FF, :].rearrange("(t p) n -> p t n", p=128)[:, qq * 8:(qq + 1) * 8, :],
                            ).then_inc(s_w2.h, 16)
                        for q in range(NQ):
                            i = eq(e, q)
                            g.dma_start(out=cwrow[:, :],
                                        in_=cwl[e * CAP + q * CH:e * CAP + (q + 1) * CH, 0:1].rearrange("n o -> o n"),
                                        ).then_inc(s_cr.h, 16)
                            for grp in range(4):
                                gi = i * 4 + grp
                                par = gi % 2
                                g.dma_start(out=idxg[:, par:par + 1],
                                            in_=lists2[(e * CAP + q * CH + grp * 128):(e * CAP + q * CH + (grp + 1) * 128), 0:1],
                                            allow_slow_non_contiguous=True).then_inc(s_ix.h, 16)
                                g.dma_start(out=drow[:, grp:grp + 1],
                                            in_=lists2[(e * CAP + q * CH + grp * 128):(e * CAP + q * CH + (grp + 1) * 128), 1:2],
                                            allow_slow_non_contiguous=True).then_inc(s_ix.h, 16)
                                g.wait_ge(s_ix.h, 32 * (gi + 1))
                                if gi >= 2:
                                    g.wait_ge(s_it.h, gi - 1)
                                g.indirect_dma_start(
                                    out=xgt[:, par * D:(par + 1) * D], out_offset=None,
                                    in_=x3tm[:, :],
                                    in_offset=bass.IndirectOffsetOnAxis(ap=idxg[:, par:par + 1], axis=0),
                                    bounds_check=rbg, oob_is_err=False).then_inc(s_gx.h, 16)
                            for sl in range(4):
                                g.wait_ge(c_ot.h, i * 8 + sl * 2 + 2)
                                g.wait_ge(s_ix.h, 32 * (i * 4 + sl + 1))
                                g.indirect_dma_start(
                                    out=moeparts[:, :],
                                    out_offset=bass.IndirectOffsetOnAxis(ap=drow[:, sl:sl + 1], axis=0),
                                    in_=etm[:, (sl % 2) * D:(sl % 2 + 1) * D], in_offset=None,
                                    bounds_check=rbs, oob_is_err=False).then_inc(s_sc.h, 16)
                    g.wait_ge(s_sc.h, 16 * NE * NQ * 4)

                @blk.tensor
                def _(t):
                    for e in range(NE):
                        t.wait_ge(s_w1.h, 64 * (e + 1))
                        for q in range(NQ):
                            i = eq(e, q)
                            for grp in range(4):
                                gi = i * 4 + grp
                                par = gi % 2
                                t.wait_ge(s_gx.h, 16 * (gi + 1))
                                if gi > 0:
                                    t.wait_ge(c_xf.h, gi)
                                for k in range(NT):
                                    tp = t.transpose(PA[:, k * 128:(k + 1) * 128],
                                                     xgt[:, par * D + k * 128:par * D + (k + 1) * 128],
                                                     cst["identb"][:, :])
                                tp.then_inc(s_it.h, 1)
                            t.wait_ge(s_cr.h, 16 * (i + 1))
                            if i > 0:
                                t.wait_ge(s_cv.h, i)
                            t.matmul(pcw[:, :], cst["onesrf"][:, :], cwrow[:, :],
                                     start=True, stop=True).then_inc(s_cb.h, 1)
                            t.wait_ge(c_xf.h, 4 * (i + 1))
                            for f in range(32):
                                pg = f % 2
                                gidx = i * 32 + f
                                if gidx >= 2:
                                    t.wait_ge(s_he.h, gidx - 1)
                                for k in range(NT):
                                    mm = t.matmul(ph[:, pg * CH:(pg + 1) * CH],
                                                  w1sb[:, k * FF + f * 128:k * FF + (f + 1) * 128],
                                                  xfm[:, k * CH:(k + 1) * CH],
                                                  start=(k == 0), stop=(k == NT - 1))
                                mm.then_inc(s_hm.h, 1)
                            t.wait_ge(s_he.h, 32 * (i + 1))
                            if q == 0:
                                t.wait_ge(s_w2.h, 64 * (e + 1))
                            for dt_ in range(8):
                                oidx = i * 8 + dt_
                                if oidx >= 2:
                                    t.wait_ge(s_fl.h, oidx - 1)
                                for k in range(32):
                                    mm = t.matmul(po[:, (dt_ % 2) * CH:(dt_ % 2 + 1) * CH],
                                                  w2sb[:, k * 1024 + dt_ * 128:k * 1024 + (dt_ + 1) * 128],
                                                  hsb[:, k * CH:(k + 1) * CH],
                                                  start=(k == 0), stop=(k == 31))
                                mm.then_inc(s_om.h, 1)
                            t.wait_ge(s_fl.h, 8 * (i + 1))
                            for sl in range(4):
                                for half in range(2):
                                    hidx = i * 8 + sl * 2 + half
                                    if hidx >= 2:
                                        t.wait_ge(c_ot.h, hidx - 1)
                                    for k4 in range(4):
                                        kk = half * 4 + k4
                                        tp = t.transpose(PT[:, k4 * 128:(k4 + 1) * 128],
                                                         tmp[:, kk * CH + sl * 128:kk * CH + (sl + 1) * 128],
                                                         cst["identb"][:, :])
                                    tp.then_inc(s_ot.h, 1)

                @blk.scalar
                def _(sc):
                    for e in range(NE):
                        for q in range(NQ):
                            i = eq(e, q)
                            for grp in range(4):
                                gi = i * 4 + grp
                                sc.wait_ge(s_it.h, gi + 1)
                                if grp == 0 and i > 0:
                                    sc.wait_ge(s_hm.h, 32 * i)
                                for k in range(NT):
                                    cp = sc.copy(xfm[:, k * CH + grp * 128:k * CH + (grp + 1) * 128],
                                                 PA[:, k * 128:(k + 1) * 128])
                                cp.then_inc(c_xf.h, 1)
                            for f in range(32):
                                pg = f % 2
                                sc.wait_ge(s_hm.h, i * 32 + f + 1)
                                sc.activation(out=hsb[:, f * CH:(f + 1) * CH],
                                              in_=ph[:, pg * CH:(pg + 1) * CH],
                                              func=AF.Relu, bias=b1cols[:, e * 32 + f:e * 32 + f + 1],
                                              scale=1.0).then_inc(s_he.h, 1)
                            for sl in range(4):
                                for half in range(2):
                                    hidx = i * 8 + sl * 2 + half
                                    sc.wait_ge(s_ot.h, hidx + 1)
                                    if half == 0 and i * 4 + sl >= 2:
                                        sc.wait_ge(s_sc.h, 16 * (i * 4 + sl - 1))
                                    for k4 in range(4):
                                        kk = half * 4 + k4
                                        cp = sc.copy(etm[:, (sl % 2) * D + kk * 128:(sl % 2) * D + (kk + 1) * 128],
                                                     PT[:, k4 * 128:(k4 + 1) * 128])
                                    cp.then_inc(c_ot.h, 1)

                @blk.vector
                def _(v):
                    for e in range(NE):
                        for q in range(NQ):
                            i = eq(e, q)
                            v.wait_ge(s_cb.h, i + 1)
                            v.tensor_copy(cwb[:, :], pcw[:, :]).then_inc(s_cv.h, 1)
                            for dt_ in range(8):
                                oidx = i * 8 + dt_
                                v.wait_ge(s_om.h, oidx + 1)
                                v.scalar_tensor_tensor(
                                    out=tmp[:, dt_ * CH:(dt_ + 1) * CH],
                                    in0=po[:, (dt_ % 2) * CH:(dt_ % 2 + 1) * CH],
                                    scalar=b2cols[:, e * 8 + dt_:e * 8 + dt_ + 1],
                                    in1=cwb[:, :], op0=OP.add, op1=OP.mult).then_inc(s_fl.h, 1)

        if not (stages & 4):
            return
        with contextlib.ExitStack() as c2:
            d1sb = c2.enter_context(nc.sbuf_tensor("m_d1", [128, NT * FF], BF16))
            d2sb = c2.enter_context(nc.sbuf_tensor("m_d2", [128, 32 * 1024], BF16))
            hsb = c2.enter_context(nc.sbuf_tensor("m_h2", [128, 32 * CH], BF16))
            xcc = c2.enter_context(nc.sbuf_tensor("m_xc2", [128, NT * CH], BF16))
            mo = c2.enter_context(nc.sbuf_tensor("m_mo", [128, 2 * 4 * CH], BF16))
            ysb = c2.enter_context(nc.sbuf_tensor("m_ysb", [128, 2 * 2 * CH], F32))
            u1 = c2.enter_context(nc.sbuf_tensor("m_u1", [128, CH], F32))
            ph = c2.enter_context(nc.psum_tensor("m_ph2", [128, 4 * CH], F32))
            po = c2.enter_context(nc.psum_tensor("m_po2", [128, 2 * CH], F32))
            prp = c2.enter_context(nc.psum_tensor("m_prp", [128, 2 * CH], F32))

            s_dw = Sem(c2, nc, "m_dw")
            s_xc = Sem(c2, nc, "m_dxc")
            s_mv = Sem(c2, nc, "m_dmv")
            s_hm = Sem(c2, nc, "m_dhm")
            s_he = Sem(c2, nc, "m_dhe")
            s_om = Sem(c2, nc, "m_dom")
            s_rp = Sem(c2, nc, "m_drp")
            s_ev = Sem(c2, nc, "m_dev")
            s_y = Sem(c2, nc, "m_dy")

            with nc.Block() as blk:
                @blk.gpsimd
                def _(g):
                    for qq in range(4):
                        g.dma_start(
                            out=d1sb[:, qq * 2 * FF:(qq + 1) * 2 * FF].rearrange("p (t n) -> p t n", t=2),
                            in_=wts["D1"].rearrange("(t p) n -> p t n", p=128)[:, qq * 2:(qq + 1) * 2, :],
                        ).then_inc(s_dw.h, 16)
                    for qq in range(4):
                        g.dma_start(
                            out=d2sb[:, qq * 8 * 1024:(qq + 1) * 8 * 1024].rearrange("p (t n) -> p t n", t=8),
                            in_=wts["D2"].rearrange("(t p) n -> p t n", p=128)[:, qq * 8:(qq + 1) * 8, :],
                        ).then_inc(s_dw.h, 16)
                    for c in range(NCH):
                        for dh in range(4):
                            oidx = c * 4 + dh
                            g.wait_ge(s_ev.h, oidx + 1)
                            g.dma_start(
                                out=y_hbm.rearrange("(t p) n -> p t n", p=128)[:, dh * 2:(dh + 1) * 2, c * CH:(c + 1) * CH],
                                in_=ysb[:, (oidx % 2) * 2 * CH:((oidx % 2) + 1) * 2 * CH].rearrange("p (t n) -> p t n", t=2),
                            ).then_inc(s_y.h, 16)
                    g.wait_ge(s_y.h, 16 * NCH * 4)

                @blk.sync
                def _(s):
                    for c in range(NCH):
                        if c >= 1:
                            s.wait_ge(s_hm.h, 16 * c)
                        s.dma_start(
                            out=xcc.rearrange("p (t n) -> p t n", t=NT),
                            in_=fmv(xbf_hbm, c * CH, (c + 1) * CH)).then_inc(s_xc.h, 16)
                        for dh in range(4):
                            oidx = c * 4 + dh
                            par = oidx % 2
                            if oidx >= 2:
                                s.wait_ge(s_ev.h, oidx - 1)
                            for m in range(2):
                                dt_ = dh * 2 + m
                                for slot in range(2):
                                    s.dma_start_transpose(
                                        mo[:, (par * 4 + slot * 2 + m) * CH:(par * 4 + slot * 2 + m + 1) * CH],
                                        moeparts[slot * TC + c * CH:slot * TC + (c + 1) * CH,
                                                 dt_ * 128:(dt_ + 1) * 128]).then_inc(s_mv.h, 16)

                @blk.tensor
                def _(t):
                    t.wait_ge(s_dw.h, 64)
                    for c in range(NCH):
                        t.wait_ge(s_xc.h, 16 * (c + 1))
                        for grp in range(16):
                            pg = grp % 2
                            gidx = c * 16 + grp
                            if gidx >= 2:
                                t.wait_ge(s_he.h, 2 * (gidx - 1))
                            for m2 in range(2):
                                f = grp * 2 + m2
                                for k in range(NT):
                                    mm = t.matmul(ph[:, (pg * 2 + m2) * CH:(pg * 2 + m2 + 1) * CH],
                                                  d1sb[:, k * FF + f * 128:k * FF + (f + 1) * 128],
                                                  xcc[:, k * CH:(k + 1) * CH],
                                                  start=(k == 0), stop=(k == NT - 1))
                            mm.then_inc(s_hm.h, 1)
                        t.wait_ge(s_he.h, 32 * (c + 1))
                        if c > 0:
                            t.wait_ge(s_ev.h, 4 * c)
                        t.matmul(prp[:, 0:CH], cst["sel10"][:, 8 * 128:9 * 128],
                                 cwrp[0:10, c * CH:(c + 1) * CH], start=True, stop=True)
                        t.matmul(prp[:, CH:2 * CH], cst["sel10"][:, 9 * 128:10 * 128],
                                 cwrp[0:10, c * CH:(c + 1) * CH], start=True, stop=True).then_inc(s_rp.h, 1)
                        for dh in range(4):
                            oidx = c * 4 + dh
                            if oidx > 0:
                                t.wait_ge(s_ev.h, oidx)
                            for m in range(2):
                                dt_ = dh * 2 + m
                                for k in range(32):
                                    mm = t.matmul(po[:, m * CH:(m + 1) * CH],
                                                  d2sb[:, k * 1024 + dt_ * 128:k * 1024 + (dt_ + 1) * 128],
                                                  hsb[:, k * CH:(k + 1) * CH],
                                                  start=(k == 0), stop=(k == 31))
                            mm.then_inc(s_om.h, 1)

                @blk.scalar
                def _(sc):
                    for c in range(NCH):
                        for grp in range(16):
                            pg = grp % 2
                            sc.wait_ge(s_hm.h, c * 16 + grp + 1)
                            for m2 in range(2):
                                f = grp * 2 + m2
                                sc.activation(out=hsb[:, f * CH:(f + 1) * CH],
                                              in_=ph[:, (pg * 2 + m2) * CH:(pg * 2 + m2 + 1) * CH],
                                              func=AF.Relu, bias=d1bcols[:, f:f + 1],
                                              scale=1.0).then_inc(s_he.h, 1)

                @blk.vector
                def _(v):
                    for c in range(NCH):
                        v.wait_ge(s_rp.h, c + 1)
                        for dh in range(4):
                            oidx = c * 4 + dh
                            par = oidx % 2
                            v.wait_ge(s_om.h, oidx + 1)
                            v.wait_ge(s_mv.h, 16 * 4 * (oidx + 1))
                            if oidx >= 2:
                                v.wait_ge(s_y.h, 16 * (oidx - 1))
                            for m in range(2):
                                dt_ = dh * 2 + m
                                ysl = ysb[:, (par * 2 + m) * CH:(par * 2 + m + 1) * CH]
                                v.tensor_scalar(out=u1[:, :], in0=po[:, m * CH:(m + 1) * CH],
                                                scalar1=d2bcols[:, dt_:dt_ + 1], scalar2=None, op0=OP.add)
                                v.tensor_tensor(u1[:, :], u1[:, :], prp[:, CH:2 * CH], OP.mult)
                                v.tensor_tensor(ysl, mo[:, (par * 4 + m) * CH:(par * 4 + m + 1) * CH],
                                                mo[:, (par * 4 + 2 + m) * CH:(par * 4 + 2 + m + 1) * CH], OP.add)
                                v.tensor_tensor(ysl, ysl, prp[:, 0:CH], OP.mult)
                                tt = v.tensor_tensor(ysl, ysl, u1[:, :], OP.add)
                            tt.then_inc(s_ev.h, 1)

WNAMES = ["L0_attn_W_write", "L0_attn_W_read", "L0_attn_W_beta", "L0_attn_decay",
          "L0_ffn_w1", "L0_ffn_b1", "L0_ffn_w2", "L0_ffn_b2",
          "L1_attn_W_write", "L1_attn_W_read", "L1_attn_decay", "L1_attn_log_alpha",
          "L1_ffn_Wr", "L1_ffn_br", "L1_ffn_Wg", "L1_ffn_bg", "L1_ffn_W1", "L1_ffn_b1",
          "L1_ffn_W2", "L1_ffn_b2", "L1_ffn_D1", "L1_ffn_d1b", "L1_ffn_D2", "L1_ffn_d2b"]

WSHAPES = {
    "L0_attn_W_write": [D, D], "L0_attn_W_read": [D, D], "L0_attn_W_beta": [8, D],
    "L0_attn_decay": [8],
    "L0_ffn_w1": [FF, D], "L0_ffn_b1": [FF], "L0_ffn_w2": [D, FF], "L0_ffn_b2": [D],
    "L1_attn_W_write": [D, D], "L1_attn_W_read": [D, D],
    "L1_attn_decay": [1, 1], "L1_attn_log_alpha": [1, 1],
    "L1_ffn_Wr": [D, 2], "L1_ffn_br": [2], "L1_ffn_Wg": [D, NE], "L1_ffn_bg": [NE],
    "L1_ffn_W1": [NE * D, FF], "L1_ffn_b1": [NE, FF], "L1_ffn_W2": [NE * FF, D],
    "L1_ffn_b2": [NE, D], "L1_ffn_D1": [D, FF], "L1_ffn_d1b": [FF],
    "L1_ffn_D2": [FF, D], "L1_ffn_d2b": [D],
}


def _decl(nc, names, kind="ExternalInput"):
    return {n: nc.dram_tensor(n, WSHAPES[n], F32, kind=kind).ap() for n in names}


def emit_cast_x(nc, xfm, xbf):
    with contextlib.ExitStack() as ctx:
        tb = ctx.enter_context(nc.sbuf_tensor("cx_t", [128, 2 * NT * CH], F32))
        bb = ctx.enter_context(nc.sbuf_tensor("cx_b", [128, 2 * NT * CH], BF16))
        s_l = Sem(ctx, nc, "cx_l")
        s_c = Sem(ctx, nc, "cx_c")
        s_o = Sem(ctx, nc, "cx_o")
        with nc.Block() as blk:
            @blk.sync
            def _(s):
                for c in range(NCH):
                    par = c % 2
                    if c >= 2:
                        s.wait_ge(s_c.h, c - 1)
                    s.dma_start(out=tb[:, par * NT * CH:(par + 1) * NT * CH].rearrange("p (t n) -> p t n", t=NT),
                                in_=fmv(xfm, c * CH, (c + 1) * CH)).then_inc(s_l.h, 16)

            @blk.vector
            def _(v):
                for c in range(NCH):
                    par = c % 2
                    v.wait_ge(s_l.h, 16 * (c + 1))
                    if c >= 2:
                        v.wait_ge(s_o.h, 16 * (c - 1))
                    v.tensor_copy(bb[:, par * NT * CH:(par + 1) * NT * CH],
                                  tb[:, par * NT * CH:(par + 1) * NT * CH]).then_inc(s_c.h, 1)

            @blk.gpsimd
            def _(g):
                for c in range(NCH):
                    par = c % 2
                    g.wait_ge(s_c.h, c + 1)
                    g.dma_start(out=fmv(xbf, c * CH, (c + 1) * CH),
                                in_=bb[:, par * NT * CH:(par + 1) * NT * CH].rearrange("p (t n) -> p t n", t=NT)).then_inc(s_o.h, 16)
                g.wait_ge(s_o.h, 16 * NCH)


def build_phase(pid, fused=False, nc=None, io=None):
    own = nc is None
    if own:
        nc = bass.Bass("TRN2", target_bir_lowering=False)
    K = "Internal" if fused else "ExternalInput"
    KO = "Internal" if fused else "ExternalOutput"
    ctx = contextlib.ExitStack()
    with ctx:
        PHASE_CONSTS = {
            0: ["onesc", "onesr", "onesrf", "identb", "eye2", "eye2r", "iota2_p1",
                "iota2_cm1", "iota2_c", "dmat2", "dmat2T", "rowc1", "rowcm1",
                "stril2", "triu2v"],
            1: ["ident", "onesc", "onesrf"],
            2: ["onesrf", "dm1T2", "striu2", "iota2_c", "iota2_cm1", "identb", "onesc"],
            3: ["ident", "sel10", "onesrf", "onesc", "onesr"],
        }
        cst = emit_load_constants(nc, ctx, PHASE_CONSTS[pid], tag=str(pid))
        if pid == 0:
            x0 = io["x0"] if fused else nc.dram_tensor("x0", [D, TC], F32, kind="ExternalInput").ap()
            w = io["w"] if fused else _decl(nc, ["L0_attn_W_write", "L0_attn_W_read", "L0_attn_W_beta", "L0_attn_decay"])
            x0b = nc.dram_tensor("x0b", [D, TC], BF16, kind="Internal").ap()
            y = nc.dram_tensor("p0_y", [D, TC], F32, kind="Internal").ap()
            scr = nc.dram_tensor("p0_scr", [D, D], BF16, kind="Internal").ap()
            x1 = io["x1"] if fused else nc.dram_tensor("x1", [D, TC], F32, kind=KO).ap()
            x1b = io["x1b"] if fused else nc.dram_tensor("x1b", [D, TC], BF16, kind=KO).ap()
            emit_cast_x(nc, x0, x0b)
            emit_l0attn(nc, cst,
                        x0b, x0,
                        {"W_write": w["L0_attn_W_write"], "W_read": w["L0_attn_W_read"],
                         "W_beta": w["L0_attn_W_beta"], "decay": w["L0_attn_decay"]},
                        y, scr)
            emit_ln(nc, cst, y, x0, x1, x1b, name="ln1")
        elif pid == 1:
            x1 = io["x1"] if fused else nc.dram_tensor("x1", [D, TC], F32, kind=K).ap()
            x1b = io["x1b"] if fused else nc.dram_tensor("x1b", [D, TC], BF16, kind=K).ap()
            w = io["w"] if fused else _decl(nc, ["L0_ffn_w1", "L0_ffn_b1", "L0_ffn_w2", "L0_ffn_b2"])
            y = nc.dram_tensor("p1_y", [D, TC], F32, kind="Internal").ap()
            scr1 = nc.dram_tensor("p1_s1", [FF, D], BF16, kind="Internal").ap()
            scr2 = nc.dram_tensor("p1_s2", [D, FF], BF16, kind="Internal").ap()
            x2 = io["x2"] if fused else nc.dram_tensor("x2", [D, TC], F32, kind=KO).ap()
            x2b = io["x2b"] if fused else nc.dram_tensor("x2b", [D, TC], BF16, kind=KO).ap()
            emit_l0ffn(nc, cst, x1b, w["L0_ffn_w1"], w["L0_ffn_b1"], w["L0_ffn_w2"],
                       w["L0_ffn_b2"], y, scr1, scr2)
            emit_ln(nc, cst, y, x1, x2, x2b, name="ln2")
        elif pid == 2:
            x2 = io["x2"] if fused else nc.dram_tensor("x2", [D, TC], F32, kind=K).ap()
            x2b = io["x2b"] if fused else nc.dram_tensor("x2b", [D, TC], BF16, kind=K).ap()
            w = io["w"] if fused else _decl(nc, ["L1_attn_W_write", "L1_attn_W_read", "L1_attn_decay", "L1_attn_log_alpha"])
            y = nc.dram_tensor("p2_y", [D, TC], F32, kind="Internal").ap()
            reads = nc.dram_tensor("p2_rd", [TC, D], BF16, kind="Internal").ap()
            scr = nc.dram_tensor("p2_scr", [D, D], BF16, kind="Internal").ap()
            x3 = io["x3"] if fused else nc.dram_tensor("x3", [D, TC], F32, kind=KO).ap()
            x3b = io["x3b"] if fused else nc.dram_tensor("x3b", [D, TC], BF16, kind=KO).ap()
            emit_l1attn(nc, cst, x2b, x2,
                        {"W_write": w["L1_attn_W_write"], "W_read": w["L1_attn_W_read"],
                         "decay": w["L1_attn_decay"], "log_alpha": w["L1_attn_log_alpha"]},
                        y, reads, scr)
            emit_ln(nc, cst, y, x2, x3, x3b, name="ln3")
        else:
            x3 = io["x3"] if fused else nc.dram_tensor("x3", [D, TC], F32, kind=K).ap()
            x3b = io["x3b"] if fused else nc.dram_tensor("x3b", [D, TC], BF16, kind=K).ap()
            w = io["w"] if fused else _decl(nc, ["L1_ffn_Wr", "L1_ffn_br", "L1_ffn_Wg", "L1_ffn_bg",
                                                 "L1_ffn_W1", "L1_ffn_b1", "L1_ffn_W2", "L1_ffn_b2",
                                                 "L1_ffn_D1", "L1_ffn_d1b", "L1_ffn_D2", "L1_ffn_d2b"])
            y = nc.dram_tensor("p3_y", [D, TC], F32, kind="Internal").ap()
            acc = nc.dram_tensor("p3_acc", [D, TC], F32, kind="Internal").ap()
            out = io["out"] if fused else nc.dram_tensor("out", [D, TC], F32, kind="ExternalOutput").ap()
            emit_l1ffn(nc, cst,
                       x3b,
                       {"Wr": w["L1_ffn_Wr"], "br": w["L1_ffn_br"], "Wg": w["L1_ffn_Wg"],
                        "bg": w["L1_ffn_bg"], "W1": w["L1_ffn_W1"], "b1": w["L1_ffn_b1"],
                        "W2": w["L1_ffn_W2"], "b2": w["L1_ffn_b2"], "D1": w["L1_ffn_D1"],
                        "d1b": w["L1_ffn_d1b"], "D2": w["L1_ffn_D2"], "d2b": w["L1_ffn_d2b"]},
                       y, acc)
            emit_ln(nc, cst, y, x3, out, None, name="ln4")
    return nc


_PHASE_IN = {
    0: (["x0"], ["L0_attn_W_write", "L0_attn_W_read", "L0_attn_W_beta", "L0_attn_decay"]),
    1: (["x1", "x1b"], ["L0_ffn_w1", "L0_ffn_b1", "L0_ffn_w2", "L0_ffn_b2"]),
    2: (["x2", "x2b"], ["L1_attn_W_write", "L1_attn_W_read", "L1_attn_decay", "L1_attn_log_alpha"]),
    3: (["x3", "x3b"], ["L1_ffn_Wr", "L1_ffn_br", "L1_ffn_Wg", "L1_ffn_bg", "L1_ffn_W1",
                        "L1_ffn_b1", "L1_ffn_W2", "L1_ffn_b2", "L1_ffn_D1", "L1_ffn_d1b",
                        "L1_ffn_D2", "L1_ffn_d2b"])}
_PHASE_OUT = {0: ["x1", "x1b"], 1: ["x2", "x2b"], 2: ["x3", "x3b"], 3: ["out"]}

_NC_CACHE = {}


def _wprep(inputs):
    wv = {}
    for n in WNAMES:
        a = np.ascontiguousarray(inputs[n])
        if n in ("L1_ffn_W1", "L1_ffn_W2"):
            a = a.reshape(-1, a.shape[-1])
        elif n in ("L1_attn_decay", "L1_attn_log_alpha"):
            a = a.reshape(1, 1)
        wv[n] = np.ascontiguousarray(a)
    return wv


def build_fused():
    nc = bass.Bass("TRN2", target_bir_lowering=False)
    nc._leaked_stacks = []

    def _phase_sems():
        st = contextlib.ExitStack()
        nc._leaked_stacks.append(st)
        nc._sem_ctx = st
    x0 = nc.dram_tensor("x0", [D, TC], F32, kind="ExternalInput").ap()
    out = nc.dram_tensor("out", [D, TC], F32, kind="ExternalOutput").ap()
    w = _decl(nc, WNAMES)
    x1 = nc.dram_tensor("x1", [D, TC], F32, kind="Internal").ap()
    x2 = nc.dram_tensor("x2", [D, TC], F32, kind="Internal").ap()
    x3 = nc.dram_tensor("x3", [D, TC], F32, kind="Internal").ap()
    x0b = nc.dram_tensor("x0b", [D, TC], BF16, kind="Internal").ap()
    x1b = nc.dram_tensor("x1b", [D, TC], BF16, kind="Internal").ap()
    x2b = nc.dram_tensor("x2b", [D, TC], BF16, kind="Internal").ap()
    x3b = nc.dram_tensor("x3b", [D, TC], BF16, kind="Internal").ap()
    y = nc.dram_tensor("ybuf", [D, TC], F32, kind="Internal").ap()
    acc = nc.dram_tensor("accbuf", [D, TC], F32, kind="Internal").ap()
    reads = nc.dram_tensor("readsbuf", [TC, D], BF16, kind="Internal").ap()
    scr_dd = nc.dram_tensor("scr_dd", [D, D], BF16, kind="Internal").ap()
    scr1 = nc.dram_tensor("scr1b", [FF, D], BF16, kind="Internal").ap()
    scr2 = nc.dram_tensor("scr2b", [D, FF], BF16, kind="Internal").ap()
    wbf = {
        "W1": nc.dram_tensor("pw1b", [NE * D, FF], BF16, kind="Internal").ap(),
        "W2": nc.dram_tensor("pw2b", [NE * FF, D], BF16, kind="Internal").ap(),
        "D1": nc.dram_tensor("pd1b", [D, FF], BF16, kind="Internal").ap(),
        "D2": nc.dram_tensor("pd2b", [FF, D], BF16, kind="Internal").ap(),
    }

    PHASE_CONSTS = {
        0: ["onesc", "onesr", "onesrf", "identb", "eye2", "eye2r", "iota2_p1",
            "iota2_cm1", "iota2_c", "dmat2", "dmat2T", "rowc1", "rowcm1",
            "stril2", "triu2v"],
        1: ["ident", "onesc", "onesrf"],
        2: ["onesrf", "dm1T2", "striu2", "iota2_c", "iota2_cm1", "identb", "onesc"],
        3: ["ident", "sel10", "onesrf", "onesc", "onesr"],
    }
    _globals = contextlib.ExitStack()
    nc._leaked_stacks.append(_globals)
    nc._sem_ctx = _globals
    s_pc = Sem(None, nc, "pcast")
    precast = []
    for e in range(NE):
        precast.append((wbf["W1"][e * D:(e + 1) * D, :], w["L1_ffn_W1"][e * D:(e + 1) * D, :]))
        precast.append((wbf["W2"][e * FF:(e + 1) * FF, :], w["L1_ffn_W2"][e * FF:(e + 1) * FF, :]))
    precast.append((wbf["D1"][:, :], w["L1_ffn_D1"][:, :]))
    precast.append((wbf["D2"][:, :], w["L1_ffn_D2"][:, :]))
    _phase_sems()
    with nc.cleanup_on_exit(), contextlib.ExitStack() as c0:
        cst = emit_load_constants(nc, c0, PHASE_CONSTS[0], tag="f0")
        emit_cast_x(nc, x0, x0b)
        emit_l0attn(nc, cst, x0b, x0,
                    {"W_write": w["L0_attn_W_write"], "W_read": w["L0_attn_W_read"],
                     "W_beta": w["L0_attn_W_beta"], "decay": w["L0_attn_decay"]},
                    y, scr_dd,
                    bz=(w["L0_ffn_w1"], scr1, w["L0_ffn_w2"], scr2),
                    ln_out=(x1, x1b))
    _phase_sems()
    with nc.cleanup_on_exit(), contextlib.ExitStack() as c1:
        cst = emit_load_constants(nc, c1, PHASE_CONSTS[1], tag="f1")
        emit_l0ffn(nc, cst, x1b, w["L0_ffn_w1"], w["L0_ffn_b1"], w["L0_ffn_w2"],
                   w["L0_ffn_b2"], y, scr1, scr2, skip_binarize=True)
        emit_ln(nc, cst, y, x1, x2, x2b, name="f_ln2")
    _phase_sems()
    with nc.cleanup_on_exit(), contextlib.ExitStack() as c2:
        cst = emit_load_constants(nc, c2, PHASE_CONSTS[2], tag="f2")
        emit_l1attn(nc, cst, x2b, x2,
                    {"W_write": w["L1_attn_W_write"], "W_read": w["L1_attn_W_read"],
                     "decay": w["L1_attn_decay"], "log_alpha": w["L1_attn_log_alpha"]},
                    y, reads, scr_dd, ln_out=(x3, x3b))
    _phase_sems()
    with nc.cleanup_on_exit(), contextlib.ExitStack() as c3:
        cst = emit_load_constants(nc, c3, PHASE_CONSTS[3], tag="f3")
        emit_l1ffn(nc, cst, x3b,
                   {"Wr": w["L1_ffn_Wr"], "br": w["L1_ffn_br"], "Wg": w["L1_ffn_Wg"],
                    "bg": w["L1_ffn_bg"], "W1": w["L1_ffn_W1"], "b1": w["L1_ffn_b1"],
                    "W2": w["L1_ffn_W2"], "b2": w["L1_ffn_b2"], "D1": w["L1_ffn_D1"],
                    "d1b": w["L1_ffn_d1b"], "D2": w["L1_ffn_D2"], "d2b": w["L1_ffn_d2b"]},
                   y, acc)
        emit_ln(nc, cst, y, x3, out, None, name="f_ln4")
    return nc


def kernel(**inputs):
    ncore = 8
    x0 = np.ascontiguousarray(inputs["L0_attn_out"])
    shards = [np.ascontiguousarray(x0[2 * c:2 * c + 2].reshape(TC, D).T) for c in range(ncore)]
    wv = _wprep(inputs)
    cstv = host_constants()
    if "fused" not in _NC_CACHE:
        _NC_CACHE["fused"] = build_fused()
    nc = _NC_CACHE["fused"]
    in_maps = []
    for c in range(ncore):
        m = {"x0": shards[c]}
        for n in WNAMES:
            m[n] = wv[n]
        for k, v in cstv.items():
            m["cst_" + k] = v
        in_maps.append(m)
    res = run_bass_kernel_spmd(nc, in_maps, core_ids=list(range(ncore)))
    out = np.stack([np.ascontiguousarray(res.results[c]["out"]).T.reshape(2, SEQ, D)
                    for c in range(ncore)])
    return out.reshape(16, SEQ, D).astype(np.float32)
